# version 08
# baseline (speedup 1.0000x reference)
import numpy as np
import ml_dtypes
import concourse.bass as bass
import concourse.mybir as mybir
from concourse.bass_utils import run_bass_kernel_spmd

F32 = mybir.dt.float32
BF16 = mybir.dt.bfloat16
FP8 = mybir.dt.float8e4
AF = mybir.ActivationFunctionType
ALU = mybir.AluOpType
AXX = mybir.AxisListType.X
BF = ml_dtypes.bfloat16

DBG = False
DBG_LEVEL = 4
PROJ_SUB = 3
PROJ_NCH = 99
DBG2 = False
B, L, D = 4, 4096, 2048
NH, CS, HF = 32, 16, 64
NPAIR = 8
SC = 256
NSC = L // SC
G = 4
GT = G * CS
NGRP = SC // GT
TOK = L // 2
NTT = TOK // 128
EPS = 1e-5


class Sched:

    def __init__(self):
        self.events = []
        self.count = {}
        self.last_wait = {}
        self.barrier_mode = False

    def op(self, eng, fn, waits=(), inc=None):
        if self.barrier_mode:
            waits = list(waits) + [(s, v) for s, v in self.count.items()]
        w = {}
        for t in waits:
            if t is None:
                continue
            sem, val = t
            if self.last_wait.get((eng, sem), -1) >= val:
                continue
            if w.get(sem, -1) < val:
                w[sem] = val
        for sem, val in w.items():
            self.last_wait[(eng, sem)] = val
        tick = None
        if inc is not None:
            sem, n = inc
            self.count[sem] = self.count.get(sem, 0) + n
            tick = (sem, self.count[sem])
        self.events.append((eng, w, fn, inc))
        return tick

    def emit(self, eng_name, eng, sems, nc=None, rec=None):
        for gid, (e, waits, fn, inc) in enumerate(self.events):
            if e != eng_name:
                continue
            for sem, val in waits.items():
                eng.wait_ge(sems[sem], val)
            i0 = nc.n_instructions() if rec is not None else 0
            instr = fn(eng)
            if inc is not None:
                sem, n = inc
                assert instr is not None, f"no instr for inc {inc}"
                instr.then_inc(sems[sem], n)
            if rec is not None:
                rec[gid] = (i0, nc.n_instructions())


def build_nc():
    nc = bass.Bass("TRN2", target_bir_lowering=False, detect_race_conditions=False)

    hT_d = nc.dram_tensor("hT", [D, L], F32, kind="ExternalInput")
    hres_d = nc.dram_tensor("hres", [TOK, D], F32, kind="ExternalInput")
    wqt_d = nc.dram_tensor("wqt", [D, 1024], F32, kind="ExternalInput")
    wkt_d = nc.dram_tensor("wkt", [D, 1024], F32, kind="ExternalInput")
    wvt_d = nc.dram_tensor("wvt", [D, 1024], F32, kind="ExternalInput")
    wilrt_d = nc.dram_tensor("wilrt", [D, 16], F32, kind="ExternalInput")
    bilr_d = nc.dram_tensor("bilr", [16, 1], F32, kind="ExternalInput")
    w1init_d = nc.dram_tensor("w1init", [128, 512], F32, kind="ExternalInput")
    wot_d = nc.dram_tensor("wot", [D, D], F32, kind="ExternalInput")
    fwt_d = nc.dram_tensor("fwt", [D, D], BF16, kind="ExternalInput")
    fbias_d = nc.dram_tensor("fbias", [128, D], BF16, kind="ExternalInput")
    negi_d = nc.dram_tensor("negi", [128, 64], BF16, kind="ExternalInput")
    mlt_d = nc.dram_tensor("mlt", [128, 512], BF16, kind="ExternalInput")
    mlm_d = nc.dram_tensor("mlm", [128, 512], BF16, kind="ExternalInput")
    mat_d = nc.dram_tensor("matt", [128, 512], BF16, kind="ExternalInput")
    mr1_d = nc.dram_tensor("mr1", [128, 512], BF16, kind="ExternalInput")
    tokf_d = nc.dram_tensor("tokf", [16, SC], F32, kind="ExternalInput")
    sel_d = nc.dram_tensor("sel", [16, 1024], F32, kind="ExternalInput")
    ones_d = nc.dram_tensor("ones1", [1, 128], F32, kind="ExternalInput")
    msel_d = nc.dram_tensor("msel", [128, 4], F32, kind="ExternalInput")
    out_d = nc.dram_tensor("out", [TOK, D], F32, kind="ExternalOutput")

    zallq = [nc.dram_tensor(f"zallq{i}", [128, 4 * 2048], BF16, kind="Internal") for i in range(4)]
    zagoq = [nc.dram_tensor(f"zagoq{i}", [256, 4 * 2048], BF16, kind="Internal") for i in range(4)]
    x1scr = nc.dram_tensor("x1scr", [TOK, D], BF16, kind="Internal")
    mx_d = nc.dram_tensor("mxin", [1, 128], F32, kind="Internal")
    dbg_d = nc.dram_tensor("dbg", [128, 40960], BF16, kind="Internal")
    mxo_d = nc.dram_tensor("mxout", [1, 128], F32, kind="Internal")

    ARENA = 96 * 1024 - 128
    keep = []
    a_cm = nc.sbuf_tensor("arena", [128, ARENA], BF16)
    arena = a_cm.__enter__()
    keep.append(a_cm)

    class Carver:
        def __init__(self, base):
            self.off = base

        def take(self, elems, dt=BF16):
            slots = elems * 2 if dt == F32 else ((elems + 1) // 2 if dt == FP8 else elems)
            o = (self.off + 31) & ~31
            self.off = o + slots
            assert self.off <= ARENA, f"arena overflow at {self.off}"
            v = arena[:, o:o + slots]
            if dt == F32:
                return v.bitcast(F32)
            if dt == FP8:
                return v.bitcast(FP8)
            return v

    g = Carver(0)
    c_negi = g.take(64)
    c_mlt = g.take(512)
    c_mlm = g.take(512)
    c_mat = g.take(512)
    c_mr1 = g.take(512)
    c_tokf = g.take(SC, F32)
    c_sel = g.take(1024, F32)
    c_w1 = g.take(512, F32)
    GB = g.off

    tail = Carver(GB)
    fw_stg = [tail.take(2048) for _ in range(2)]
    mx_s = tail.take(8, F32)
    wmx_s = tail.take(8, F32)
    st_s = tail.take(64, F32)
    st4_s = tail.take(64, F32)
    sc_s = tail.take(64, F32)
    fb_s = tail.take(2048)
    bl_s = tail.take(8, F32)
    onesv_s = tail.take(128, F32)
    msel_s = tail.take(4, F32)
    eps_s = tail.take(8, F32)
    OV = tail.off

    p1 = Carver(OV)
    wq_s = p1.take(16 * 1024)
    wk_s = p1.take(16 * 1024)
    wv_s = p1.take(16 * 1024)
    wilr_s = p1.take(16 * 16)
    ht_s = [p1.take(16 * SC) for _ in range(2)]
    xbt_s = [p1.take(NPAIR * SC) for _ in range(2)]
    xct_s = [p1.take(NPAIR * SC) for _ in range(2)]
    xat_s = [p1.take(NPAIR * SC) for _ in range(2)]
    xbco_s = [p1.take(NPAIR * SC) for _ in range(2)]
    xcco_s = [p1.take(NPAIR * SC) for _ in range(2)]
    co_s = p1.take(SC, F32)
    cox_s = p1.take(SC, F32)
    g_u = p1.take(512)
    g_lm = p1.take(512)
    g_ltm = p1.take(512)
    g_att = p1.take(512)
    g_r1 = p1.take(512)
    g_xbn = p1.take(512)
    g_lt2 = p1.take(512)
    g_t1n = p1.take(512)
    g_z1 = p1.take(512)
    g_w = p1.take(512)
    zst = [p1.take(512) for _ in range(2)]
    P1E = p1.off

    p2 = Carver(OV)
    wot_s = p2.take(16 * 2048)
    ztl_s = [p2.take(1024) for _ in range(4)]
    ztp_s = [p2.take(1024) for _ in range(4)]
    ztt_s = [p2.take(1024) for _ in range(4)]
    hr_s = [p2.take(2048, F32) for _ in range(4)]
    x1p_s = [p2.take(2048, F32) for _ in range(2)]
    x1b_s = [p2.take(2048) for _ in range(2)]
    sq2_s = [p2.take(2048, F32) for _ in range(2)]
    P2E = p2.off

    p4 = Carver(OV)
    wq8_s = p4.take(16 * 2048, FP8)
    xq8_s = p4.take(16 * 2048, FP8)
    y_s = [p4.take(2048) for _ in range(2)]
    x1r_s = [p4.take(2048) for _ in range(2)]
    x2p_s = [p4.take(2048, F32) for _ in range(2)]
    o_s = [p4.take(2048, F32) for _ in range(2)]
    sq4_s = [p4.take(2048, F32) for _ in range(2)]
    P4E = p4.off
    assert max(P1E, P2E, P4E) <= ARENA, (P1E, P2E, P4E)

    x1t_cm = nc.sbuf_tensor("x1t", [128, 4 * 2048], BF16, is_transpose_dest=True)
    x1t = x1t_cm.__enter__()
    keep.append(x1t_cm)

    ps = []
    for i in range(8):
        cm = nc.psum_tensor(f"ps{i}", [128, 512], F32)
        ps.append(cm.__enter__())
        keep.append(cm)

    sems = {}
    for s in ["dA", "dB", "dC0", "dC1", "dZ0", "dZ1", "dX", "dM", "dO",
              "dG", "dH0", "dH1", "dF0", "dF1", "dT0", "dT1", "dR0", "dR1",
              "sT", "sV", "sS", "sC"]:
        cm = nc.semaphore(s)
        sems[s] = cm.__enter__()
        keep.append(cm)

    S = Sched()
    import os
    if os.environ.get("KSERIAL", "0") in ("2", "full"):
        S.barrier_mode = True

    def dma(dst, src):
        return lambda e: e.dma_start(out=dst, in_=src)

    t_c = None
    for dst, src in [(c_negi[:, :], negi_d.ap()), (c_mlt[:, :], mlt_d.ap()),
                     (c_mlm[:, :], mlm_d.ap()), (c_mat[:, :], mat_d.ap()),
                     (c_mr1[:, :], mr1_d.ap()), (c_tokf[0:16, :], tokf_d.ap()),
                     (c_sel[0:16, :], sel_d.ap()), (c_w1[:, :], w1init_d.ap()),
                     (bl_s[0:16, 0:1], bilr_d.ap()), (onesv_s[0:1, :], ones_d.ap()),
                     (msel_s[:, :], msel_d.ap()), (fb_s[:, :], fbias_d.ap())]:
        t_c = S.op("sync", dma(dst, src), inc=("dA", 16))
    t_const = t_c

    def load_weight(dst_s, src_d, ncols, nsplit=2):
        t = None
        src = src_d.ap().rearrange("(k r) j -> r k j", r=128)
        dst = dst_s.rearrange("r (k j) -> r k j", k=16)
        step = 16 // nsplit
        for i in range(nsplit):
            t = S.op("gpsimd", dma(dst[:, step * i:step * (i + 1), :],
                                   src[:, step * i:step * (i + 1), :]),
                     inc=("dG", 16))
        return t

    load_weight(wq_s, wqt_d, 1024)
    load_weight(wk_s, wkt_d, 1024)
    load_weight(wv_s, wvt_d, 1024)
    tw_all = load_weight(wilr_s, wilrt_d, 16, nsplit=1)
    tw_q = tw_k = tw_v = tw_i = tw_all
    t_wsb0 = S.op("vector", lambda e: e.tensor_copy(g_w[:, :], c_w1[:, :]),
                  waits=[t_const], inc=("sV", 1))
    t_eps = S.op("vector", lambda e: e.memset(eps_s[:, 0:1], EPS), inc=("sV", 1))

    ht_ready = [None] * (NSC + 1)
    proj_done = [None] * (NSC + 1)
    x_ready = [None] * (NSC + 1)
    scan_s3_done = [None] * (NSC + 1)
    scan_s1_done = [None] * (NSC + 1)
    w_ready = t_wsb0
    last_z_evac = None
    zdma = []
    coll_ticks = []

    def emit_ht_load(s):
        par = s % 2
        sem = "dH%d" % par
        src = hT_d.ap().rearrange("(k r) t -> r k t", r=128)[:, :, SC * s:SC * (s + 1)]
        dst = ht_s[par].rearrange("r (k t) -> r k t", k=16)
        t = None
        for i in range(2):
            t = S.op("gpsimd", dma(dst[:, 8 * i:8 * (i + 1), :], src[:, 8 * i:8 * (i + 1), :]),
                     waits=[proj_done[s - 2] if s >= 2 else tw_all],
                     inc=(sem, 16))
        ht_ready[s] = t

    def gen_proj(s):
        par = s % 2
        deps = [ht_ready[s], tw_q, tw_k, tw_v, tw_i, t_const]
        evq = []
        ev_all = []
        mm_last = None
        for wi, (w_s, dst2) in enumerate([(wk_s, xbt_s[par]), (wq_s, xct_s[par]), (wv_s, None)]):
            for p in range(NPAIR):
                ci = wi * NPAIR + p
                if DBG and ci >= PROJ_NCH:
                    continue
                bank = ps[ci % 2]

                def mmchain(e, w_s=w_s, p=p, bank=bank, par=par):
                    mm = None
                    for k in range(16):
                        mm = e.matmul(bank[:, 0:SC],
                                      w_s[:, 1024 * k + 128 * p:1024 * k + 128 * p + 128],
                                      ht_s[par][:, SC * k:SC * (k + 1)],
                                      start=(k == 0), stop=(k == 15))
                    return mm

                w = list(deps) + [evq[-2] if len(evq) >= 2 else None]
                if p == 0:
                    w.append(scan_s3_done[s - 2] if s >= 2 else None)
                    w.append(scan_s1_done[s - 2] if s >= 2 else None)
                t_mm = S.op("tensor", mmchain, waits=w, inc=("sT", 1))
                mm_last = t_mm
                dst = dst2[:, SC * p:SC * (p + 1)] if dst2 is not None else xat_s[par][:, SC * p:SC * (p + 1)]
                t_ev = S.op("vector",
                            (lambda bank, dst: (lambda e: e.tensor_copy(dst, bank)))(bank[:, 0:SC], dst),
                            waits=[t_mm], inc=("sV", 1))
                evq.append(t_ev)
                ev_all.append(t_ev)
                yield
        proj_mm_last = mm_last
        if DBG and PROJ_SUB < 2:
            x_ready[s] = evq[-1]
            proj_done[s] = proj_mm_last
            return

        def ilrchain(e, par=par):
            mm = None
            for k in range(16):
                mm = e.matmul(ps[7][0:16, 0:SC], wilr_s[:, 16 * k:16 * (k + 1)],
                              ht_s[par][:, SC * k:SC * (k + 1)], start=(k == 0), stop=(k == 15))
            return mm

        t_ilr = S.op("tensor", ilrchain, waits=deps + [x_ready[s - 1] if s >= 1 else None],
                     inc=("sT", 1))
        t_sig = S.op("scalar",
                     lambda e: e.activation(co_s[0:16, :], ps[7][0:16, 0:SC], AF.Sigmoid,
                                            bias=bl_s[0:16, 0:1]),
                     waits=[t_ilr, x_ready[s - 1] if s >= 1 else None], inc=("sS", 1))
        t_co = S.op("vector",
                    lambda e: e.tensor_tensor(co_s[0:16, :], co_s[0:16, :], c_tokf[0:16, :], op=ALU.mult),
                    waits=[t_sig, t_const], inc=("sV", 1))
        last = t_co
        if DBG and PROJ_SUB < 3:
            x_ready[s] = t_co
            proj_done[s] = proj_mm_last
            return
        yield
        for p in range(NPAIR):
            t_sm = S.op("tensor",
                        (lambda p: (lambda e: e.matmul(ps[7][:, 0:SC], c_sel[0:16, 128 * p:128 * (p + 1)],
                                                       co_s[0:16, :], start=True, stop=True)))(p),
                        waits=[t_co, last], inc=("sT", 1))
            t_cx = S.op("vector", lambda e: e.tensor_copy(cox_s[:, :], ps[7][:, 0:SC]),
                        waits=[t_sm], inc=("sV", 1))
            t_xb = S.op("vector",
                        (lambda p, par: (lambda e: e.tensor_tensor(
                            xbco_s[par][:, SC * p:SC * (p + 1)], xbt_s[par][:, SC * p:SC * (p + 1)],
                            cox_s[:, :], op=ALU.mult)))(p, par),
                        waits=[t_cx, ev_all[p]], inc=("sV", 1))
            t_xc = S.op("vector",
                        (lambda p, par: (lambda e: e.tensor_tensor(
                            xcco_s[par][:, SC * p:SC * (p + 1)], xct_s[par][:, SC * p:SC * (p + 1)],
                            cox_s[:, :], op=ALU.mult)))(p, par),
                        waits=[t_xb, ev_all[NPAIR + p]], inc=("sV", 1))
            last = t_xc
            yield
        x_ready[s] = last
        proj_done[s] = t_ilr

    def gen_scan(s):
        nonlocal w_ready, last_z_evac
        par = s % 2
        xb, xc = xbt_s[par], xct_s[par]
        xa, xbcov, xccov = xat_s[par], xbco_s[par], xcco_s[par]
        s3_last = None
        s1_last = None
        for gi in range(NGRP):
            if DBG and gi > 0:
                break
            gI = s * NGRP + gi
            tc = GT * gi

            def xsl(t, h, tc=tc):
                sl, p = h % 2, h // 2
                return t[64 * sl:64 * sl + 64, SC * p + tc:SC * p + tc + GT]

            mm1 = []
            for h in range(NHL := 16):
                sl, p = h % 2, h // 2
                pb, pf = 64 * sl, 64 * p

                def f(e, h=h, pb=pb, pf=pf, xsl=xsl):
                    st, sk, sp = h < 2, h % 2 == 1, h >= 14
                    e.matmul(ps[2][pb:pb + 64, pf:pf + 64], xsl(xb, h),
                             g_w[pb:pb + 64, pf:pf + 64], start=st, stop=False,
                             skip_group_check=sk)
                    e.matmul(ps[2][pb:pb + 64, pf:pf + 64], xsl(xa, h),
                             c_negi[pb:pb + 64, :], start=False, stop=sp,
                             skip_group_check=sk)
                    e.matmul(ps[3][pb:pb + 64, pf:pf + 64], xsl(xbcov, h),
                             xsl(xb, h), start=st, stop=sp, skip_group_check=sk)
                    e.matmul(ps[4][pb:pb + 64, pf:pf + 64], xsl(xb, h),
                             xsl(xbcov, h), start=st, stop=sp, skip_group_check=sk)
                    return e.matmul(ps[5][pb:pb + 64, pf:pf + 64], xsl(xb, h),
                                    xsl(xccov, h), start=st, stop=sp, skip_group_check=sk)

                w = [x_ready[s], w_ready, last_z_evac] if h == 0 else []
                mm1.append(S.op("tensor", f, waits=w, inc=("sT", 1) if h == 15 else None))
            t_mm1 = mm1[-1]
            t_ltm = S.op("vector", lambda e: e.tensor_tensor(g_ltm[:, :], ps[3][:, :], c_mlt[:, :], op=ALU.mult),
                         waits=[t_mm1], inc=("sV", 1))
            t_lm = S.op("vector", lambda e: e.tensor_tensor(g_lm[:, :], ps[4][:, :], c_mlm[:, :], op=ALU.mult),
                        waits=[t_mm1], inc=("sV", 1))
            t_att = S.op("vector", lambda e: e.tensor_tensor(g_att[:, :], ps[5][:, :], c_mat[:, :], op=ALU.mult),
                         waits=[t_mm1], inc=("sV", 1))
            t_u = S.op("scalar", lambda e: e.activation(g_u[:, :], ps[2][:, :], AF.Copy),
                       waits=[t_mm1], inc=("sS", 1))
            yield

            mm2 = []
            for h in range(16):
                sl, p = h % 2, h // 2
                pb, pf = 64 * sl, 64 * p

                def f(e, h=h, pb=pb, pf=pf, xsl=xsl):
                    st, sk, sp = h < 2, h % 2 == 1, h >= 14
                    e.matmul(ps[4][pb:pb + 64, pf:pf + 64], xsl(xbcov, h),
                             xsl(xc, h), start=st, stop=sp, skip_group_check=sk)
                    e.matmul(ps[3][pb:pb + 64, pf:pf + 64], xsl(xbcov, h),
                             c_negi[pb:pb + 64, :], start=st, stop=sp, skip_group_check=sk)
                    return e.matmul(ps[5][pb:pb + 64, pf:pf + 64], g_lm[pb:pb + 64, pf:pf + 64],
                                    g_ltm[pb:pb + 64, pf:pf + 64], start=st, stop=sp,
                                    skip_group_check=sk)

                w = [t_ltm, t_lm, t_att, t_u] if h == 0 else []
                mm2.append(S.op("tensor", f, waits=w, inc=("sT", 1) if h == 15 else None))
            t_mm2 = mm2[-1]
            s1_last = t_mm2
            t_r1 = S.op("vector", lambda e: e.tensor_tensor(g_r1[:, :], ps[4][:, :], c_mr1[:, :], op=ALU.mult),
                        waits=[t_mm2], inc=("sV", 1))
            t_xbn = S.op("scalar", lambda e: e.activation(g_xbn[:, :], ps[3][:, :], AF.Copy),
                         waits=[t_mm2], inc=("sS", 1))
            t_lt2 = S.op("scalar", lambda e: e.activation(g_lt2[:, :], ps[5][:, :], AF.Copy),
                         waits=[t_mm2], inc=("sS", 1))
            yield

            mm3 = []
            for h in range(16):
                sl, p = h % 2, h // 2
                pb, pf = 64 * sl, 64 * p

                def f(e, h=h, pb=pb, pf=pf):
                    return e.matmul(ps[2][pb:pb + 64, pf:pf + 64], g_lt2[pb:pb + 64, pf:pf + 64],
                                    g_u[pb:pb + 64, pf:pf + 64], start=False, stop=False,
                                    skip_group_check=True)

                mm3.append(S.op("tensor", f, waits=[t_lt2, t_u] if h == 0 else [],
                                inc=("sT", 1) if h == 15 else None))
            t_t1n = S.op("scalar", lambda e: e.activation(g_t1n[:, :], ps[2][:, :], AF.Copy, scale=-1.0),
                         waits=[mm3[-1]], inc=("sS", 1))
            yield
            mm4 = []
            for h in range(16):
                sl, p = h % 2, h // 2
                pb, pf = 64 * sl, 64 * p

                def f(e, h=h, pb=pb, pf=pf):
                    return e.matmul(ps[2][pb:pb + 64, pf:pf + 64], g_ltm[pb:pb + 64, pf:pf + 64],
                                    g_t1n[pb:pb + 64, pf:pf + 64], start=False, stop=(h >= 14),
                                    skip_group_check=True)

                mm4.append(S.op("tensor", f, waits=[t_t1n] if h == 0 else [],
                                inc=("sT", 1) if h == 15 else None))
            t_z1 = S.op("scalar", lambda e: e.activation(g_z1[:, :], ps[2][:, :], AF.Copy),
                        waits=[mm4[-1]], inc=("sS", 1))
            yield

            mm5 = []
            for h in range(16):
                sl, p = h % 2, h // 2
                pb, pf = 64 * sl, 64 * p

                def f(e, h=h, pb=pb, pf=pf, xsl=xsl, gI=gI):
                    st, sk, sp = h < 2, h % 2 == 1, h >= 14
                    e.matmul(ps[2][pb:pb + 64, pf:pf + 64], g_w[pb:pb + 64, pf:pf + 64],
                             xsl(xc, h), start=st, stop=False, skip_group_check=sk)
                    e.matmul(ps[2][pb:pb + 64, pf:pf + 64], g_z1[pb:pb + 64, pf:pf + 64],
                             g_att[pb:pb + 64, pf:pf + 64], start=False, stop=False,
                             skip_group_check=sk)
                    e.matmul(ps[2][pb:pb + 64, pf:pf + 64], g_z1[pb:pb + 64, pf:pf + 64],
                             g_r1[pb:pb + 64, pf:pf + 64], start=False, stop=sp,
                             skip_group_check=sk)
                    return e.matmul(ps[6][pb:pb + 64, pf:pf + 64], g_xbn[pb:pb + 64, pf:pf + 64],
                                    g_z1[pb:pb + 64, pf:pf + 64],
                                    start=(gI == 0 and h < 2), stop=(h >= 14),
                                    skip_group_check=(sk or gI > 0))

                w = [t_z1, t_xbn, t_r1] if h == 0 else []
                mm5.append(S.op("tensor", f, waits=w, inc=("sT", 1) if h == 15 else None))
            t_mm5 = mm5[-1]
            s3_last = t_mm5
            zpar = gI % 2
            t_zt = S.op("vector",
                        (lambda zpar: (lambda e: e.tensor_copy(zst[zpar][:, :], ps[2][:, :])))(zpar),
                        waits=[t_mm5, zdma[-2] if len(zdma) >= 2 else None], inc=("sV", 1))
            last_z_evac = t_zt
            zdst = zallq[s // 4].ap().rearrange("r (sc p tau) -> r sc p tau", sc=4, p=NPAIR)[
                :, s % 4, :, GT * gi:GT * (gi + 1)]
            zsrc = zst[zpar][:, :].rearrange("r (p tau) -> r p tau", p=NPAIR)
            zdma.append(S.op("gpsimd", (lambda zd, zs: (lambda e: e.dma_start(out=zd, in_=zs)))(zdst, zsrc),
                             waits=[t_zt], inc=("dZ%d" % zpar, 16)))
            w_ready = S.op("vector", lambda e: e.tensor_tensor(g_w[:, :], c_w1[:, :], ps[6][:, :], op=ALU.add),
                           waits=[t_mm5], inc=("sV", 1))
            yield
        scan_s3_done[s] = s3_last
        scan_s1_done[s] = s1_last
        if s % 4 == 3:
            q = s // 4
            coll_ticks.append(S.op("gpsimd",
                (lambda q: (lambda e: e.collective_compute(
                    "AllGather", ALU.bypass,
                    replica_groups=[[0, 1], [2, 3], [4, 5], [6, 7]],
                    ins=[zallq[q].ap().opt()], outs=[zagoq[q].ap().opt()])))(q),
                waits=[zdma[-1]], inc=("sC", 1)))

    if DBG:
        if DBG_LEVEL >= 2:
            emit_ht_load(0)
        if DBG_LEVEL >= 3:
            emit_proj(0)
        if DBG_LEVEL >= 4:
            emit_scan(0)
        dumps = [wk_s[:, 15 * 1024:16 * 1024]]
        if DBG_LEVEL >= 2:
            dumps += [ht_s[0]]
        if DBG_LEVEL >= 3:
            dumps += [xbt_s[0], xct_s[0], xat_s, xbco_s, xcco_s]
        if DBG_LEVEL >= 4:
            dumps += [g_u, g_lm, g_ltm, g_att, g_r1, g_xbn, g_lt2, g_t1n, g_z1,
                      zst[0], g_w]
        off = 0
        t_dmp = None
        for dsrc in dumps:
            ncol = dsrc.shape[-1]
            nrow = dsrc.shape[0]
            dwaits = [tw_q, tw_k, tw_v, tw_i, t_wsb0, t_dmp]
            if DBG_LEVEL >= 2:
                dwaits.append(ht_ready[0])
            if DBG_LEVEL >= 3:
                dwaits.append(x_ready[0])
            if DBG_LEVEL >= 4:
                dwaits += [w_ready, last_z_evac, zdma[-1]]
            t_dmp = S.op("gpsimd",
                         (lambda dsrc, off, ncol, nrow: (lambda e: e.dma_start(
                             out=dbg_d.ap()[0:nrow, off:off + ncol], in_=dsrc)))(dsrc, off, ncol, nrow),
                         waits=dwaits, inc=("dZ", 16))
            off += ncol
        with nc.Block() as blk:
            @blk.tensor
            def _(eng):
                S.emit("tensor", eng, sems)

            @blk.vector
            def _(eng):
                S.emit("vector", eng, sems)

            @blk.scalar
            def _(eng):
                S.emit("scalar", eng, sems)

            @blk.sync
            def _(eng):
                S.emit("sync", eng, sems)

            @blk.gpsimd
            def _(eng):
                S.emit("gpsimd", eng, sems)
                eng.wait_ge(sems["dZ"], t_dmp[1])
        return nc

    emit_ht_load(0)
    for _ in gen_proj(0):
        pass
    for s in range(NSC):
        if s + 1 < NSC:
            emit_ht_load(s + 1)
        gs = gen_scan(s)
        gp = gen_proj(s + 1) if s + 1 < NSC else None
        k = 0
        while True:
            try:
                next(gs)
            except StopIteration:
                break
            if gp is not None:
                for _ in range(2 if k % 3 != 2 else 1):
                    try:
                        next(gp)
                    except StopIteration:
                        gp = None
                        break
            k += 1
        if gp is not None:
            for _ in gp:
                pass

    import os
    if os.environ.get("KSERIAL", "0") == "1":
        S.barrier_mode = True

    p1_free = S.op("vector", lambda e: e.tensor_copy(sc_s[0:1, 0:1], c_w1[0:1, 0:1]),
                   waits=[proj_done[NSC - 1]], inc=("sV", 1))
    p1_free2 = [proj_done[NSC - 1], scan_s3_done[NSC - 1], scan_s1_done[NSC - 1],
                w_ready, last_z_evac, x_ready[NSC - 1]]
    wsrc = wot_d.ap().rearrange("(k r) j -> r k j", r=128)
    wdst = wot_s.rearrange("r (k j) -> r k j", k=16)
    t_wot = None
    for i in range(4):
        t_wot = S.op("gpsimd", dma(wdst[:, 4 * i:4 * (i + 1), :], wsrc[:, 4 * i:4 * (i + 1), :]),
                     waits=[p1_free], inc=("dB", 16))

    t_wmax = S.op("vector", lambda e: e.memset(wmx_s[:, 0:1], 0.0), inc=("sV", 1))
    fw_prev = [None, None]
    for k in range(16):
        sp = k % 2
        t_l = S.op("sync", dma(fw_stg[sp][:, :], fwt_d.ap()[128 * k:128 * (k + 1), :]),
                   waits=[fw_prev[sp]], inc=("dF%d" % sp, 16))
        t_m = S.op("vector",
                   (lambda sp: (lambda e: e.reduce_max(wmx_s[:, 1:2], fw_stg[sp][:, :], axis=AXX,
                                                       apply_absolute_value=True)))(sp),
                   waits=[t_l, t_wmax], inc=("sV", 1))
        t_wmax = S.op("vector", lambda e: e.tensor_tensor(wmx_s[:, 0:1], wmx_s[:, 0:1], wmx_s[:, 1:2], op=ALU.max),
                      waits=[t_m], inc=("sV", 1))
        fw_prev[sp] = t_wmax

    t_mxacc = S.op("vector", lambda e: e.memset(mx_s[:, 0:1], 0.0), inc=("sV", 1))
    x1_ticks = []
    prev_sq = None
    hist = []

    def zview(bufs, row, X, tt):
        sc = X * (NSC // 2) + tt // 2
        v = bufs[sc // 4].ap().rearrange("r (c p tau) -> r c p tau", p=NPAIR, tau=SC)
        return v[row:row + 128, sc % 4, :, (tt % 2) * 128:(tt % 2) * 128 + 128]

    def zcoll(X, tt):
        return coll_ticks[(X * (NSC // 2) + tt // 2) // 4]

    import os as _os
    _G1 = _os.environ.get("KG", "")
    for tt in range(NTT):
        par = tt % 2
        zpar4 = tt % 4
        h2 = hist[tt - 2] if tt >= 2 else None
        h4 = hist[tt - 4] if tt >= 4 else None
        guard = h4["mm"] if h4 else None
        gw = [guard] if h4 else list(p1_free2)
        sem = "dC%d" % par
        t_l1 = S.op("sync", dma(ztl_s[zpar4][:, :].rearrange("r (p c) -> r p c", p=NPAIR), zview(zallq, 0, 0, tt)),
                    waits=gw, inc=(sem, 16))
        t_l2 = S.op("sync", dma(ztt_s[zpar4][:, :].rearrange("r (p c) -> r p c", p=NPAIR), zview(zallq, 0, 1, tt)),
                    waits=gw, inc=(sem, 16))
        t_p1 = S.op("sync", dma(ztp_s[zpar4][:, :].rearrange("r (p c) -> r p c", p=NPAIR), zview(zagoq, 128, 0, tt)),
                    waits=[zcoll(0, tt)] + gw, inc=(sem, 16))
        t_hr = S.op("sync", dma(hr_s[zpar4][:, :], hres_d.ap()[128 * tt:128 * (tt + 1), :]),
                    waits=gw, inc=(sem, 16))
        t_o1 = S.op("vector",
                    (lambda q4: (lambda e: e.tensor_scalar(ztl_s[q4][:, :], ztl_s[q4][:, :],
                                                           msel_s[:, 0:1], None, op0=ALU.mult)))(zpar4),
                    waits=[t_hr], inc=("sV", 1))
        t_own = S.op("vector",
                     (lambda q4: (lambda e: e.scalar_tensor_tensor(ztl_s[q4][:, :], ztt_s[q4][:, :],
                                                                   msel_s[:, 1:2], ztl_s[q4][:, :],
                                                                   op0=ALU.mult, op1=ALU.add)))(zpar4),
                     waits=[t_o1], inc=("sV", 1))
        t_q1 = S.op("vector",
                    (lambda q4: (lambda e: e.tensor_scalar(ztp_s[q4][:, :], ztp_s[q4][:, :],
                                                           msel_s[:, 2:3], None, op0=ALU.mult)))(zpar4),
                    waits=[t_hr], inc=("sV", 1))
        t_p2 = S.op("sync", dma(ztt_s[zpar4][:, :].rearrange("r (p c) -> r p c", p=NPAIR), zview(zagoq, 0, 1, tt)),
                    waits=[t_own, zcoll(1, tt)], inc=(sem, 16))
        t_par = S.op("vector",
                     (lambda q4: (lambda e: e.scalar_tensor_tensor(ztp_s[q4][:, :], ztt_s[q4][:, :],
                                                                   msel_s[:, 3:4], ztp_s[q4][:, :],
                                                                   op0=ALU.mult, op1=ALU.add)))(zpar4),
                     waits=[t_q1, t_p2], inc=("sV", 1))
        mmt = []
        for ob in range(4):
            def f(e, ob=ob, par=par, tt=tt):
                bank = ps[par * 4 + ob]
                mm = None
                for jj in range(16):
                    lhs = (ztl_s[tt % 4] if jj < 8 else ztp_s[tt % 4])[:, 128 * (jj % 8):128 * (jj % 8) + 128]
                    mm = e.matmul(bank[:, :], lhs,
                                  wot_s[:, 2048 * jj + 512 * ob:2048 * jj + 512 * ob + 512],
                                  start=(jj == 0), stop=(jj == 15))
                return mm

            w = [t_own, t_par, t_wot, h2["ev"] if h2 else None]
            if tt == 0:
                w += [w_ready, last_z_evac]
            mmt.append(S.op("tensor", f, waits=w, inc=("sT", 1)))
        ev = []
        for ob in range(4):
            def f(e, ob=ob, par=par, q4=zpar4):
                return e.tensor_tensor(x1p_s[par][:, 512 * ob:512 * (ob + 1)],
                                       ps[par * 4 + ob][:, :],
                                       hr_s[q4][:, 512 * ob:512 * (ob + 1)], op=ALU.add)

            w = [mmt[ob], t_hr]
            if h2:
                w += [h2["x1"], h2["sqr"]]
            ev.append(S.op("vector", f, waits=w, inc=("sV", 1)))

        t_sqr = S.op("scalar",
                     (lambda par: (lambda e: e.activation(sq2_s[par][:, :], x1p_s[par][:, :], AF.Square,
                                                          accum_out=st_s[:, 1:2])))(par),
                     waits=ev + [prev_sq], inc=("sS", 1))
        t_st = S.op("vector",
                    (lambda par: (lambda e: e.reduce_sum(st_s[:, 0:1], x1p_s[par][:, :], axis=AXX)))(par),
                    waits=ev + [prev_sq], inc=("sV", 1))
        t_mn = S.op("vector", lambda e: e.tensor_scalar(st_s[:, 24:25], st_s[:, 0:1], 1.0 / D, None, op0=ALU.mult),
                    waits=[t_st], inc=("sV", 1))
        t_m2 = S.op("vector", lambda e: e.tensor_tensor(st_s[:, 29:30], st_s[:, 24:25], st_s[:, 24:25], op=ALU.mult),
                    waits=[t_mn], inc=("sV", 1))
        t_vr = S.op("vector", lambda e: e.scalar_tensor_tensor(st_s[:, 25:26], st_s[:, 1:2], 1.0 / D,
                                                               st_s[:, 29:30], op0=ALU.mult, op1=ALU.subtract),
                    waits=[t_m2, t_sqr], inc=("sV", 1))
        t_sq = S.op("scalar", lambda e: e.activation(st_s[:, 26:27], st_s[:, 25:26], AF.Sqrt, bias=eps_s[:, 0:1]),
                    waits=[t_vr, t_eps], inc=("sS", 1))
        prev_sq = t_sq
        t_rc = S.op("vector", lambda e: e.reciprocal(st_s[:, 27:28], st_s[:, 26:27]),
                    waits=[t_sq], inc=("sV", 1))
        t_nm = S.op("vector", lambda e: e.scalar_tensor_tensor(st_s[:, 28:29], st_s[:, 24:25], -1.0,
                                                               st_s[:, 27:28], op0=ALU.mult, op1=ALU.mult),
                    waits=[t_rc], inc=("sV", 1))
        t_x1 = S.op("vector",
                    (lambda par: (lambda e: e.tensor_scalar(x1b_s[par][:, :], x1p_s[par][:, :],
                                                            st_s[:, 27:28], st_s[:, 28:29],
                                                            op0=ALU.mult, op1=ALU.add)))(par),
                    waits=[t_nm, h2["xd"] if h2 else None], inc=("sV", 1))
        t_m = S.op("vector",
                   (lambda par: (lambda e: e.reduce_max(mx_s[:, 1:2], x1b_s[par][:, :], axis=AXX,
                                                        apply_absolute_value=True)))(par),
                   waits=[t_x1, t_mxacc], inc=("sV", 1))
        t_mxacc = S.op("vector", lambda e: e.tensor_tensor(mx_s[:, 0:1], mx_s[:, 0:1], mx_s[:, 1:2], op=ALU.max),
                       waits=[t_m], inc=("sV", 1))
        t_xd = S.op("gpsimd",
                    (lambda par, tt: (lambda e: e.dma_start(out=x1scr.ap()[128 * tt:128 * (tt + 1), :],
                                                            in_=x1b_s[par][:, :])))(par, tt),
                    waits=[t_x1], inc=("dX", 16))
        x1_ticks.append(t_xd)
        hist.append({"mm": mmt[-1], "ev": ev[-1], "x1": t_x1, "sqr": t_sqr, "xd": t_xd})

    if os.environ.get("KSERIAL", "0") == "3":
        S.barrier_mode = True
    t_mxd = S.op("gpsimd", lambda e: e.dma_start(out=mx_d.ap().rearrange("o c -> c o"), in_=mx_s[:, 0:1]),
                 waits=[t_mxacc], inc=("dM", 16))
    t_ar = S.op("gpsimd",
                lambda e: e.collective_compute("AllReduce", ALU.max, replica_groups=[list(range(8))],
                                               ins=[mx_d.ap().opt()], outs=[mxo_d.ap().opt()]),
                waits=[t_mxd], inc=("sC", 1))
    t_l = S.op("sync", dma(sc_s[0:8, 16:32], mxo_d.ap().rearrange("o (a b) -> (o a) b", b=16)),
               waits=[t_ar], inc=("dA", 16))
    t_r = S.op("vector", lambda e: e.reduce_max(sc_s[0:8, 32:33], sc_s[0:8, 16:32], axis=AXX),
               waits=[t_l], inc=("sV", 1))
    t_d = S.op("gpsimd", lambda e: e.dma_start(out=mx_d.ap()[0:1, 0:8].rearrange("o c -> c o"),
                                               in_=sc_s[0:8, 32:33]),
               waits=[t_r], inc=("dM", 16))
    t_l2 = S.op("sync", dma(sc_s[0:1, 33:41], mx_d.ap()[0:1, 0:8]), waits=[t_d], inc=("dA", 16))
    t_xs = S.op("vector", lambda e: e.reduce_max(sc_s[0:1, 8:9], sc_s[0:1, 33:41], axis=AXX),
                waits=[t_l2], inc=("sV", 1))
    t_xs = S.op("vector", lambda e: e.tensor_scalar(sc_s[0:1, 8:9], sc_s[0:1, 8:9], 1e-6, None, op0=ALU.max),
                waits=[t_xs], inc=("sV", 1))
    t_wd = S.op("gpsimd", lambda e: e.dma_start(out=mx_d.ap().rearrange("o c -> c o"),
                                                in_=wmx_s[:, 0:1]),
                waits=[t_wmax, t_l, t_l2], inc=("dM", 16))
    t_wl = S.op("sync", dma(sc_s[0:8, 16:32], mx_d.ap().rearrange("o (a b) -> (o a) b", b=16)),
                waits=[t_wd, t_r], inc=("dA", 16))
    t_wr = S.op("vector", lambda e: e.reduce_max(sc_s[0:8, 32:33], sc_s[0:8, 16:32], axis=AXX),
                waits=[t_wl, t_xs], inc=("sV", 1))
    t_wd2 = S.op("gpsimd", lambda e: e.dma_start(out=mx_d.ap()[0:1, 0:8].rearrange("o c -> c o"),
                                                 in_=sc_s[0:8, 32:33]),
                 waits=[t_wr, t_l2], inc=("dM", 16))
    t_wl2 = S.op("sync", dma(sc_s[0:1, 33:41], mx_d.ap()[0:1, 0:8]), waits=[t_wd2], inc=("dA", 16))
    t_ws = S.op("vector", lambda e: e.reduce_max(sc_s[0:1, 9:10], sc_s[0:1, 33:41], axis=AXX),
                waits=[t_wl2], inc=("sV", 1))
    t_ws = S.op("vector", lambda e: e.tensor_scalar(sc_s[0:1, 9:10], sc_s[0:1, 9:10], 1e-6, None, op0=ALU.max),
                waits=[t_ws], inc=("sV", 1))
    t_rws = S.op("vector", lambda e: e.reciprocal(sc_s[0:1, 11:12], sc_s[0:1, 9:10]),
                 waits=[t_ws], inc=("sV", 1))
    t_bcwm = S.op("tensor", lambda e: e.matmul(ps[7][:, 4:5], onesv_s[0:1, 0:128],
                                               sc_s[0:1, 11:12], start=True, stop=True),
                  waits=[t_rws, t_const], inc=("sT", 1))
    t_bcw = S.op("vector", lambda e: e.tensor_copy(sc_s[:, 21:22], ps[7][:, 4:5]),
                 waits=[t_bcwm], inc=("sV", 1))
    t_rxs = S.op("vector", lambda e: e.reciprocal(sc_s[0:1, 10:11], sc_s[0:1, 8:9]),
                 waits=[t_xs], inc=("sV", 1))
    t_xw = S.op("vector", lambda e: e.tensor_tensor(sc_s[0:1, 12:13], sc_s[0:1, 8:9],
                                                    sc_s[0:1, 9:10], op=ALU.mult),
                waits=[t_xs, t_ws], inc=("sV", 1))
    t_bcm = S.op("tensor", lambda e: e.matmul(ps[7][:, 0:3], onesv_s[0:1, 0:128],
                                              sc_s[0:1, 10:13], start=True, stop=True),
                 waits=[t_rxs, t_xw, t_const], inc=("sT", 1))
    t_bc = S.op("vector", lambda e: e.tensor_copy(sc_s[:, 24:27], ps[7][:, 0:3]),
                waits=[t_bcm], inc=("sV", 1))

    if os.environ.get("KSERIAL", "0") == "4":
        S.barrier_mode = True
    wq_ticks = []
    for k in range(16):
        sp = k % 2
        t_lw = S.op("sync", dma(fw_stg[sp][:, :], fwt_d.ap()[128 * k:128 * (k + 1), :]),
                    waits=[wq_ticks[-2] if len(wq_ticks) >= 2 else t_bcw], inc=("dF%d" % sp, 16))
        t_q = S.op("scalar",
                   (lambda sp, k: (lambda e: e.activation(wq8_s[:, 2048 * k:2048 * (k + 1)],
                                                          fw_stg[sp][:, :], AF.Copy,
                                                          scale=sc_s[:, 21:22])))(sp, k),
                   waits=[t_lw, t_bcw], inc=("sS", 1))
        wq_ticks.append(t_q)
    t_wq = wq_ticks[-1]

    xq_ticks = []
    for k in range(16):
        xpar = k % 4
        t_t = S.op("sync",
                   (lambda k, xpar: (lambda e: e.dma_start_transpose(
                       x1t[:, 2048 * xpar:2048 * (xpar + 1)], x1scr.ap()[:, 128 * k:128 * (k + 1)])))(k, xpar),
                   waits=[x1_ticks[-1], xq_ticks[-4] if len(xq_ticks) >= 4 else None],
                   inc=("dT%d" % (xpar % 2), 16))
        eng = "scalar" if k % 2 == 0 else "vector"
        if eng == "scalar":
            t_q = S.op(eng,
                       (lambda k, xpar: (lambda e: e.activation(xq8_s[:, 2048 * k:2048 * (k + 1)],
                                                                x1t[:, 2048 * xpar:2048 * (xpar + 1)], AF.Copy,
                                                                scale=sc_s[:, 24:25])))(k, xpar),
                       waits=[t_t, t_bc], inc=("sS", 1))
        else:
            t_q = S.op(eng,
                       (lambda k, xpar: (lambda e: e.tensor_scalar(xq8_s[:, 2048 * k:2048 * (k + 1)],
                                                                   x1t[:, 2048 * xpar:2048 * (xpar + 1)],
                                                                   sc_s[:, 24:25], None, op0=ALU.mult)))(k, xpar),
                       waits=[t_t, t_bc], inc=("sV", 1))
        xq_ticks.append(t_q)
    t_xq = xq_ticks[-1]
    t_xq2 = xq_ticks[-2]

    out_ticks = []
    prev_sq4 = None
    for tt in range(NTT):
        par = tt % 2
        guard = out_ticks[-2] if len(out_ticks) >= 2 else None
        t_x1r = S.op("sync", dma(x1r_s[par][:, :], x1scr.ap()[128 * tt:128 * (tt + 1), :]),
                     waits=[x1_ticks[-1], guard], inc=("dR%d" % par, 16))
        mmt = []
        for ob in range(4):
            def f(e, ob=ob, par=par, tt=tt):
                bank = ps[par * 4 + ob]
                mm = None
                for k in range(16):
                    mm = e.matmul(bank[:, :],
                                  xq8_s[:, 2048 * k + 128 * tt:2048 * k + 128 * tt + 128],
                                  wq8_s[:, 2048 * k + 512 * ob:2048 * k + 512 * ob + 512],
                                  start=(k == 0), stop=(k == 15))
                return mm

            w = [t_wq, t_xq, t_xq2, guard]
            if tt == 0:
                w.append(t_bcm)
            mmt.append(S.op("tensor", f, waits=w, inc=("sT", 1)))
        ev = []
        for ob in range(4):
            def f1(e, ob=ob, par=par):
                return e.tensor_scalar(y_s[par][:, 512 * ob:512 * (ob + 1)], ps[par * 4 + ob][:, :],
                                       sc_s[:, 26:27], None, op0=ALU.mult)

            t_y = S.op("vector", f1, waits=[mmt[ob], t_bc], inc=("sV", 1))

            def f2(e, ob=ob, par=par):
                return e.tensor_tensor(y_s[par][:, 512 * ob:512 * (ob + 1)],
                                       y_s[par][:, 512 * ob:512 * (ob + 1)],
                                       fb_s[:, 512 * ob:512 * (ob + 1)], op=ALU.add)

            t_y2 = S.op("vector", f2, waits=[t_y, t_const], inc=("sV", 1))

            def f3(e, ob=ob, par=par):
                return e.tensor_tensor(x2p_s[par][:, 512 * ob:512 * (ob + 1)],
                                       y_s[par][:, 512 * ob:512 * (ob + 1)],
                                       x1r_s[par][:, 512 * ob:512 * (ob + 1)], op=ALU.add)

            ev.append(S.op("vector", f3, waits=[t_y2, t_x1r], inc=("sV", 1)))

        t_sqr = S.op("scalar",
                     (lambda par: (lambda e: e.activation(sq4_s[par][:, :], x2p_s[par][:, :], AF.Square,
                                                          accum_out=st4_s[:, 1:2])))(par),
                     waits=ev + [prev_sq4], inc=("sS", 1))
        t_st = S.op("vector",
                    (lambda par: (lambda e: e.reduce_sum(st4_s[:, 0:1], x2p_s[par][:, :], axis=AXX)))(par),
                    waits=ev + [prev_sq4], inc=("sV", 1))
        t_mn = S.op("vector", lambda e: e.tensor_scalar(st4_s[:, 24:25], st4_s[:, 0:1], 1.0 / D, None, op0=ALU.mult),
                    waits=[t_st], inc=("sV", 1))
        t_m2 = S.op("vector", lambda e: e.tensor_tensor(st4_s[:, 29:30], st4_s[:, 24:25], st4_s[:, 24:25], op=ALU.mult),
                    waits=[t_mn], inc=("sV", 1))
        t_vr = S.op("vector", lambda e: e.scalar_tensor_tensor(st4_s[:, 25:26], st4_s[:, 1:2], 1.0 / D,
                                                               st4_s[:, 29:30], op0=ALU.mult, op1=ALU.subtract),
                    waits=[t_m2, t_sqr], inc=("sV", 1))
        t_sq = S.op("scalar", lambda e: e.activation(st4_s[:, 26:27], st4_s[:, 25:26], AF.Sqrt, bias=eps_s[:, 0:1]),
                    waits=[t_vr, t_eps], inc=("sS", 1))
        prev_sq4 = t_sq
        t_rc = S.op("vector", lambda e: e.reciprocal(st4_s[:, 27:28], st4_s[:, 26:27]),
                    waits=[t_sq], inc=("sV", 1))
        t_nm = S.op("vector", lambda e: e.scalar_tensor_tensor(st4_s[:, 28:29], st4_s[:, 24:25], -1.0,
                                                               st4_s[:, 27:28], op0=ALU.mult, op1=ALU.mult),
                    waits=[t_rc], inc=("sV", 1))
        t_o = S.op("vector",
                   (lambda par: (lambda e: e.tensor_scalar(o_s[par][:, :], x2p_s[par][:, :],
                                                           st4_s[:, 27:28], st4_s[:, 28:29],
                                                           op0=ALU.mult, op1=ALU.add)))(par),
                   waits=[t_nm], inc=("sV", 1))
        t_od = S.op("gpsimd",
                    (lambda par, tt: (lambda e: e.dma_start(out=out_d.ap()[128 * tt:128 * (tt + 1), :],
                                                            in_=o_s[par][:, :])))(par, tt),
                    waits=[t_o], inc=("dO", 16))
        out_ticks.append(t_od)

    final = out_ticks[-1]

    if DBG2:
        zall_o = nc.dram_tensor("zall_o", [128, NSC * 2048], BF16, kind="ExternalOutput")
        x1_o = nc.dram_tensor("x1_o", [TOK, D], BF16, kind="ExternalOutput")
        scl_o = nc.dram_tensor("scl_o", [128, 64], F32, kind="ExternalOutput")
        zago_o = nc.dram_tensor("zago_o", [256, NSC * 2048], BF16, kind="ExternalOutput")
        t_d1 = S.op("gpsimd", lambda e: e.dma_start(out=zall_o.ap(), in_=zall.ap()),
                    waits=[zdma[-1], out_ticks[-1]], inc=("dZ", 16))
        t_d2 = S.op("gpsimd", lambda e: e.dma_start(out=x1_o.ap(), in_=x1scr.ap()),
                    waits=[t_d1], inc=("dZ", 16))
        t_d3 = S.op("gpsimd", lambda e: e.dma_start(out=zago_o.ap(), in_=zago.ap()),
                    waits=[t_d2], inc=("dZ", 16))
        t_d4 = S.op("gpsimd", lambda e: e.dma_start(out=scl_o.ap(), in_=sc_s[:, 0:64].bitcast(BF16).bitcast(F32)),
                    waits=[t_bc, t_d3], inc=("dZ", 16))
        final = t_d4
        fsem = "dZ"
    else:
        fsem = "dO"

    rec = {} if os.environ.get("KREC", "0") == "1" else None
    with nc.Block() as blk:
        @blk.tensor
        def _(eng):
            S.emit("tensor", eng, sems, nc, rec)

        @blk.vector
        def _(eng):
            S.emit("vector", eng, sems, nc, rec)

        @blk.scalar
        def _(eng):
            S.emit("scalar", eng, sems, nc, rec)

        @blk.sync
        def _(eng):
            S.emit("sync", eng, sems, nc, rec)

        @blk.gpsimd
        def _(eng):
            S.emit("gpsimd", eng, sems, nc, rec)
            eng.wait_ge(sems[fsem], final[1])
    nc._sched = S
    nc._rec = rec

    return nc


def _consts():
    negi = np.tile(-np.eye(64, dtype=np.float32), (2, 1)).astype(BF)
    ci = np.arange(GT) // CS
    up = (ci[None, :] > ci[:, None]).astype(np.float32)
    tri = ((ci[:, None] == ci[None, :]) &
           (np.arange(GT)[:, None] <= np.arange(GT)[None, :])).astype(np.float32)

    def tile(m):
        o = np.zeros((128, 512), np.float32)
        for sl in range(2):
            for p in range(8):
                o[64 * sl:64 * sl + 64, 64 * p:64 * p + 64] = m
        return o.astype(BF)

    mlt, mlm, matt, mr1 = tile(up), tile(up.T), tile(-tri), tile(-up)
    tokf = np.tile((1.0 / ((np.arange(SC) % CS) + 1.0) / HF).astype(np.float32)[None, :], (16, 1))
    sel = np.zeros((16, 1024), np.float32)
    for h in range(16):
        p, sl = h // 2, h % 2
        sel[h, 128 * p + 64 * sl:128 * p + 64 * sl + 64] = 1.0
    return negi, mlt, mlm, matt, mr1, np.ascontiguousarray(tokf), np.ascontiguousarray(sel), np.ones((1, 128), np.float32)


def kernel(L0_attn_hidden_states, L0_attn_Wq, L0_attn_Wk, L0_attn_Wv, L0_attn_Wo,
           L0_attn_W_ilr, L0_attn_b_ilr, L0_attn_W1, L0_ffn_weight, L0_ffn_bias):
    hid = np.ascontiguousarray(np.asarray(L0_attn_hidden_states, np.float32))
    Wq = np.asarray(L0_attn_Wq, np.float32)
    Wk = np.asarray(L0_attn_Wk, np.float32)
    Wv = np.asarray(L0_attn_Wv, np.float32)
    Wo = np.asarray(L0_attn_Wo, np.float32)
    Wilr = np.asarray(L0_attn_W_ilr, np.float32)
    bilr = np.asarray(L0_attn_b_ilr, np.float32)
    W1 = np.asarray(L0_attn_W1, np.float32)
    fw = np.asarray(L0_ffn_weight)
    fb = np.asarray(L0_ffn_bias)

    negi, mlt, mlm, matt, mr1, tokf, sel, ones1 = _consts()
    fwt = np.ascontiguousarray(fw.T)
    fbias = np.ascontiguousarray(np.tile(np.asarray(fb)[None, :], (128, 1)))

    in_maps = []
    for c in range(8):
        b, hh = c // 2, c % 2
        Hs = slice(1024 * hh, 1024 * (hh + 1))
        w1i = np.zeros((128, 512), np.float32)
        for hl in range(16):
            p, sl = hl // 2, hl % 2
            w1i[64 * sl:64 * sl + 64, 64 * p:64 * p + 64] = W1[16 * hh + hl]
        wot = np.ascontiguousarray(np.concatenate(
            [Wo[:, Hs].T, Wo[:, 1024 * (1 - hh):1024 * (2 - hh)].T], axis=0))
        msel = np.zeros((128, 4), np.float32)
        msel[:, 0] = 1.0 if hh == 0 else 0.0
        msel[:, 1] = 1.0 if hh == 1 else 0.0
        msel[:, 2] = 1.0 if hh == 0 else 0.0
        msel[:, 3] = 1.0 if hh == 1 else 0.0
        in_maps.append({
            "hT": np.ascontiguousarray(hid[b].T),
            "hres": np.ascontiguousarray(hid[b][TOK * hh:TOK * (hh + 1), :]),
            "wqt": np.ascontiguousarray(Wq[Hs, :].T),
            "wkt": np.ascontiguousarray(Wk[Hs, :].T),
            "wvt": np.ascontiguousarray(Wv[Hs, :].T),
            "wilrt": np.ascontiguousarray(Wilr[16 * hh:16 * hh + 16, :].T),
            "bilr": np.ascontiguousarray(bilr[16 * hh:16 * hh + 16, None]),
            "w1init": w1i, "wot": wot, "fwt": fwt, "fbias": fbias,
            "negi": negi, "mlt": mlt, "mlm": mlm, "matt": matt, "mr1": mr1,
            "tokf": tokf, "sel": sel, "ones1": ones1, "msel": msel,
        })

    nc = build_nc()
    res = run_bass_kernel_spmd(nc, in_maps, core_ids=list(range(8)))
    out = np.zeros((B, L, D), np.float32)
    for c in range(8):
        b, hh = c // 2, c % 2
        out[b, TOK * hh:TOK * (hh + 1), :] = res.results[c]["out"]
    return out


# version 09
# speedup vs baseline: 1.0817x; 1.0817x over previous
import numpy as np
import ml_dtypes
import concourse.bass as bass
import concourse.mybir as mybir
from concourse.bass_utils import run_bass_kernel_spmd

F32 = mybir.dt.float32
BF16 = mybir.dt.bfloat16
FP8 = mybir.dt.float8e4
AF = mybir.ActivationFunctionType
ALU = mybir.AluOpType
AXX = mybir.AxisListType.X
BF = ml_dtypes.bfloat16

DBG = False
DBG_LEVEL = 4
PROJ_SUB = 3
PROJ_NCH = 99
DBG2 = False
B, L, D = 4, 4096, 2048
NH, CS, HF = 32, 16, 64
NPAIR = 8
SC = 256
NSC = L // SC
G = 4
GT = G * CS
NGRP = SC // GT
TOK = L // 2
NTT = TOK // 128
EPS = 1e-5


class Sched:

    def __init__(self):
        self.events = []
        self.count = {}
        self.last_wait = {}
        self.barrier_mode = False

    def op(self, eng, fn, waits=(), inc=None):
        if self.barrier_mode:
            waits = list(waits) + [(s, v) for s, v in self.count.items()]
        w = {}
        for t in waits:
            if t is None:
                continue
            sem, val = t
            if self.last_wait.get((eng, sem), -1) >= val:
                continue
            if w.get(sem, -1) < val:
                w[sem] = val
        for sem, val in w.items():
            self.last_wait[(eng, sem)] = val
        tick = None
        if inc is not None:
            sem, n = inc
            self.count[sem] = self.count.get(sem, 0) + n
            tick = (sem, self.count[sem])
        self.events.append((eng, w, fn, inc))
        return tick

    def emit(self, eng_name, eng, sems, nc=None, rec=None):
        for gid, (e, waits, fn, inc) in enumerate(self.events):
            if e != eng_name:
                continue
            for sem, val in waits.items():
                eng.wait_ge(sems[sem], val)
            i0 = nc.n_instructions() if rec is not None else 0
            instr = fn(eng)
            if inc is not None:
                sem, n = inc
                assert instr is not None, f"no instr for inc {inc}"
                instr.then_inc(sems[sem], n)
            if rec is not None:
                rec[gid] = (i0, nc.n_instructions())


def build_nc():
    nc = bass.Bass("TRN2", target_bir_lowering=False, detect_race_conditions=False)

    hT_d = nc.dram_tensor("hT", [D, L], F32, kind="ExternalInput")
    hres_d = nc.dram_tensor("hres", [TOK, D], F32, kind="ExternalInput")
    wqt_d = nc.dram_tensor("wqt", [D, 1024], F32, kind="ExternalInput")
    wkt_d = nc.dram_tensor("wkt", [D, 1024], F32, kind="ExternalInput")
    wvt_d = nc.dram_tensor("wvt", [D, 1024], F32, kind="ExternalInput")
    wilrt_d = nc.dram_tensor("wilrt", [D, 16], F32, kind="ExternalInput")
    bilr_d = nc.dram_tensor("bilr", [16, 1], F32, kind="ExternalInput")
    w1init_d = nc.dram_tensor("w1init", [128, 512], F32, kind="ExternalInput")
    wot_d = nc.dram_tensor("wot", [D, D], F32, kind="ExternalInput")
    fwt_d = nc.dram_tensor("fwt", [D, D], BF16, kind="ExternalInput")
    fbias_d = nc.dram_tensor("fbias", [128, D], BF16, kind="ExternalInput")
    negi_d = nc.dram_tensor("negi", [128, 64], BF16, kind="ExternalInput")
    mlt_d = nc.dram_tensor("mlt", [128, 512], BF16, kind="ExternalInput")
    mlm_d = nc.dram_tensor("mlm", [128, 512], BF16, kind="ExternalInput")
    mat_d = nc.dram_tensor("matt", [128, 512], BF16, kind="ExternalInput")
    mr1_d = nc.dram_tensor("mr1", [128, 512], BF16, kind="ExternalInput")
    tokf_d = nc.dram_tensor("tokf", [16, SC], F32, kind="ExternalInput")
    sel_d = nc.dram_tensor("sel", [16, 1024], F32, kind="ExternalInput")
    ones_d = nc.dram_tensor("ones1", [1, 128], F32, kind="ExternalInput")
    msel_d = nc.dram_tensor("msel", [128, 4], F32, kind="ExternalInput")
    out_d = nc.dram_tensor("out", [TOK, D], F32, kind="ExternalOutput")

    zallq = [nc.dram_tensor(f"zallq{i}", [128, 4 * 2048], BF16, kind="Internal") for i in range(4)]
    zagoq = [nc.dram_tensor(f"zagoq{i}", [256, 4 * 2048], BF16, kind="Internal") for i in range(4)]
    x1scr = nc.dram_tensor("x1scr", [TOK, D], BF16, kind="Internal")
    mx_d = nc.dram_tensor("mxin", [1, 128], F32, kind="Internal")
    dbg_d = nc.dram_tensor("dbg", [128, 40960], BF16, kind="Internal")
    mxo_d = nc.dram_tensor("mxout", [1, 128], F32, kind="Internal")

    ARENA = 96 * 1024 - 128
    keep = []
    a_cm = nc.sbuf_tensor("arena", [128, ARENA], BF16)
    arena = a_cm.__enter__()
    keep.append(a_cm)

    class Carver:
        def __init__(self, base):
            self.off = base

        def take(self, elems, dt=BF16):
            slots = elems * 2 if dt == F32 else ((elems + 1) // 2 if dt == FP8 else elems)
            o = (self.off + 31) & ~31
            self.off = o + slots
            assert self.off <= ARENA, f"arena overflow at {self.off}"
            v = arena[:, o:o + slots]
            if dt == F32:
                return v.bitcast(F32)
            if dt == FP8:
                return v.bitcast(FP8)
            return v

    g = Carver(0)
    c_negi = g.take(64)
    c_mlt = g.take(512)
    c_mlm = g.take(512)
    c_mat = g.take(512)
    c_mr1 = g.take(512)
    c_tokf = g.take(SC, F32)
    c_sel = g.take(1024, F32)
    c_w1 = g.take(512, F32)
    GB = g.off

    tail = Carver(GB)
    fw_stg = [tail.take(2048) for _ in range(2)]
    mx_s = tail.take(8, F32)
    wmx_s = tail.take(8, F32)
    st_s = tail.take(64, F32)
    st4_s = tail.take(64, F32)
    sc_s = tail.take(64, F32)
    fb_s = tail.take(2048)
    bl_s = tail.take(8, F32)
    onesv_s = tail.take(128, F32)
    msel_s = tail.take(4, F32)
    eps_s = tail.take(8, F32)
    OV = tail.off

    p1 = Carver(OV)
    wq_s = p1.take(16 * 1024)
    wk_s = p1.take(16 * 1024)
    wv_s = p1.take(16 * 1024)
    wilr_s = p1.take(16 * 16)
    ht_s = [p1.take(16 * SC) for _ in range(2)]
    xbt_s = [p1.take(NPAIR * SC) for _ in range(2)]
    xct_s = [p1.take(NPAIR * SC) for _ in range(2)]
    xat_s = [p1.take(NPAIR * SC) for _ in range(2)]
    xbco_s = [p1.take(NPAIR * SC) for _ in range(2)]
    xcco_s = [p1.take(NPAIR * SC) for _ in range(2)]
    co_s = p1.take(SC, F32)
    cox_s = p1.take(SC, F32)
    g_u = p1.take(512)
    g_lm = p1.take(512)
    g_ltm = p1.take(512)
    g_att = p1.take(512)
    g_r1 = p1.take(512)
    g_xbn = p1.take(512)
    g_lt2 = p1.take(512)
    g_t1n = p1.take(512)
    g_z1 = p1.take(512)
    g_w = p1.take(512)
    zst = [p1.take(512) for _ in range(2)]
    P1E = p1.off

    p2 = Carver(OV)
    wot_s = p2.take(16 * 2048)
    ztl_s = [p2.take(1024) for _ in range(4)]
    ztp_s = [p2.take(1024) for _ in range(4)]
    ztt_s = [p2.take(1024) for _ in range(4)]
    hr_s = [p2.take(2048, F32) for _ in range(4)]
    x1p_s = [p2.take(2048, F32) for _ in range(2)]
    x1b_s = [p2.take(2048) for _ in range(2)]
    sq2_s = [p2.take(2048, F32) for _ in range(2)]
    P2E = p2.off

    p4 = Carver(OV)
    wq8_s = p4.take(16 * 2048, FP8)
    xq8_s = p4.take(16 * 2048, FP8)
    y_s = [p4.take(2048) for _ in range(2)]
    x1r_s = [p4.take(2048) for _ in range(2)]
    x2p_s = [p4.take(2048, F32) for _ in range(2)]
    o_s = [p4.take(2048, F32) for _ in range(2)]
    sq4_s = [p4.take(2048, F32) for _ in range(2)]
    P4E = p4.off
    assert max(P1E, P2E, P4E) <= ARENA, (P1E, P2E, P4E)

    x1t_cm = nc.sbuf_tensor("x1t", [128, 4 * 2048], BF16, is_transpose_dest=True)
    x1t = x1t_cm.__enter__()
    keep.append(x1t_cm)

    ps = []
    for i in range(8):
        cm = nc.psum_tensor(f"ps{i}", [128, 512], F32)
        ps.append(cm.__enter__())
        keep.append(cm)

    sems = {}
    for s in ["dA", "dB", "dC0", "dC1", "dZ0", "dZ1", "dX", "dM", "dO",
              "dG", "dH0", "dH1", "dF0", "dF1", "dT0", "dT1", "dR0", "dR1",
              "sT", "sV", "sS", "sC"]:
        cm = nc.semaphore(s)
        sems[s] = cm.__enter__()
        keep.append(cm)

    S = Sched()
    import os
    if os.environ.get("KSERIAL", "0") in ("2", "full"):
        S.barrier_mode = True

    def dma(dst, src):
        return lambda e: e.dma_start(out=dst, in_=src)

    t_c = None
    for dst, src in [(c_negi[:, :], negi_d.ap()), (c_mlt[:, :], mlt_d.ap()),
                     (c_mlm[:, :], mlm_d.ap()), (c_mat[:, :], mat_d.ap()),
                     (c_mr1[:, :], mr1_d.ap()), (c_tokf[0:16, :], tokf_d.ap()),
                     (c_sel[0:16, :], sel_d.ap()), (c_w1[:, :], w1init_d.ap()),
                     (bl_s[0:16, 0:1], bilr_d.ap()), (onesv_s[0:1, :], ones_d.ap()),
                     (msel_s[:, :], msel_d.ap()), (fb_s[:, :], fbias_d.ap())]:
        t_c = S.op("sync", dma(dst, src), inc=("dA", 16))
    t_const = t_c

    def load_weight(dst_s, src_d, ncols, nsplit=2):
        t = None
        src = src_d.ap().rearrange("(k r) j -> r k j", r=128)
        dst = dst_s.rearrange("r (k j) -> r k j", k=16)
        step = 16 // nsplit
        for i in range(nsplit):
            t = S.op("gpsimd", dma(dst[:, step * i:step * (i + 1), :],
                                   src[:, step * i:step * (i + 1), :]),
                     inc=("dG", 16))
        return t

    load_weight(wq_s, wqt_d, 1024)
    load_weight(wk_s, wkt_d, 1024)
    load_weight(wv_s, wvt_d, 1024)
    tw_all = load_weight(wilr_s, wilrt_d, 16, nsplit=1)
    tw_q = tw_k = tw_v = tw_i = tw_all
    t_wsb0 = S.op("vector", lambda e: e.tensor_copy(g_w[:, :], c_w1[:, :]),
                  waits=[t_const], inc=("sV", 1))
    t_eps = S.op("vector", lambda e: e.memset(eps_s[:, 0:1], EPS), inc=("sV", 1))

    ht_ready = [None] * (NSC + 1)
    proj_done = [None] * (NSC + 1)
    x_ready = [None] * (NSC + 1)
    scan_s3_done = [None] * (NSC + 1)
    scan_s1_done = [None] * (NSC + 1)
    w_ready = t_wsb0
    last_z_evac = None
    zdma = []
    coll_ticks = []

    def emit_ht_load(s):
        par = s % 2
        sem = "dH%d" % par
        src = hT_d.ap().rearrange("(k r) t -> r k t", r=128)[:, :, SC * s:SC * (s + 1)]
        dst = ht_s[par].rearrange("r (k t) -> r k t", k=16)
        t = None
        for i in range(2):
            t = S.op("gpsimd", dma(dst[:, 8 * i:8 * (i + 1), :], src[:, 8 * i:8 * (i + 1), :]),
                     waits=[proj_done[s - 2] if s >= 2 else tw_all],
                     inc=(sem, 16))
        ht_ready[s] = t

    def gen_proj(s):
        par = s % 2
        deps = [ht_ready[s], tw_q, tw_k, tw_v, tw_i, t_const]
        evq = []
        ev_all = []
        mm_last = None
        for wi, (w_s, dst2) in enumerate([(wk_s, xbt_s[par]), (wq_s, xct_s[par]), (wv_s, None)]):
            for p in range(NPAIR):
                ci = wi * NPAIR + p
                if DBG and ci >= PROJ_NCH:
                    continue
                bank = ps[ci % 2]

                def mmchain(e, w_s=w_s, p=p, bank=bank, par=par):
                    mm = None
                    for k in range(16):
                        mm = e.matmul(bank[:, 0:SC],
                                      w_s[:, 1024 * k + 128 * p:1024 * k + 128 * p + 128],
                                      ht_s[par][:, SC * k:SC * (k + 1)],
                                      start=(k == 0), stop=(k == 15))
                    return mm

                w = list(deps) + [evq[-2] if len(evq) >= 2 else None]
                if p == 0:
                    w.append(scan_s3_done[s - 2] if s >= 2 else None)
                    w.append(scan_s1_done[s - 2] if s >= 2 else None)
                t_mm = S.op("tensor", mmchain, waits=w, inc=("sT", 1))
                mm_last = t_mm
                dst = dst2[:, SC * p:SC * (p + 1)] if dst2 is not None else xat_s[par][:, SC * p:SC * (p + 1)]
                t_ev = S.op("vector",
                            (lambda bank, dst: (lambda e: e.tensor_copy(dst, bank)))(bank[:, 0:SC], dst),
                            waits=[t_mm], inc=("sV", 1))
                evq.append(t_ev)
                ev_all.append(t_ev)
                yield
        proj_mm_last = mm_last
        if DBG and PROJ_SUB < 2:
            x_ready[s] = evq[-1]
            proj_done[s] = proj_mm_last
            return

        def ilrchain(e, par=par):
            mm = None
            for k in range(16):
                mm = e.matmul(ps[7][0:16, 0:SC], wilr_s[:, 16 * k:16 * (k + 1)],
                              ht_s[par][:, SC * k:SC * (k + 1)], start=(k == 0), stop=(k == 15))
            return mm

        t_ilr = S.op("tensor", ilrchain, waits=deps + [x_ready[s - 1] if s >= 1 else None],
                     inc=("sT", 1))
        t_sig = S.op("scalar",
                     lambda e: e.activation(co_s[0:16, :], ps[7][0:16, 0:SC], AF.Sigmoid,
                                            bias=bl_s[0:16, 0:1]),
                     waits=[t_ilr, x_ready[s - 1] if s >= 1 else None], inc=("sS", 1))
        t_co = S.op("vector",
                    lambda e: e.tensor_tensor(co_s[0:16, :], co_s[0:16, :], c_tokf[0:16, :], op=ALU.mult),
                    waits=[t_sig, t_const], inc=("sV", 1))
        last = t_co
        if DBG and PROJ_SUB < 3:
            x_ready[s] = t_co
            proj_done[s] = proj_mm_last
            return
        yield
        for p in range(NPAIR):
            t_sm = S.op("tensor",
                        (lambda p: (lambda e: e.matmul(ps[7][:, 0:SC], c_sel[0:16, 128 * p:128 * (p + 1)],
                                                       co_s[0:16, :], start=True, stop=True)))(p),
                        waits=[t_co, last], inc=("sT", 1))
            t_cx = S.op("vector", lambda e: e.tensor_copy(cox_s[:, :], ps[7][:, 0:SC]),
                        waits=[t_sm], inc=("sV", 1))
            t_xb = S.op("vector",
                        (lambda p, par: (lambda e: e.tensor_tensor(
                            xbco_s[par][:, SC * p:SC * (p + 1)], xbt_s[par][:, SC * p:SC * (p + 1)],
                            cox_s[:, :], op=ALU.mult)))(p, par),
                        waits=[t_cx, ev_all[p]], inc=("sV", 1))
            t_xc = S.op("vector",
                        (lambda p, par: (lambda e: e.tensor_tensor(
                            xcco_s[par][:, SC * p:SC * (p + 1)], xct_s[par][:, SC * p:SC * (p + 1)],
                            cox_s[:, :], op=ALU.mult)))(p, par),
                        waits=[t_xb, ev_all[NPAIR + p]], inc=("sV", 1))
            last = t_xc
            yield
        x_ready[s] = last
        proj_done[s] = t_ilr

    def gen_scan(s):
        nonlocal w_ready, last_z_evac
        par = s % 2
        xb, xc = xbt_s[par], xct_s[par]
        xa, xbcov, xccov = xat_s[par], xbco_s[par], xcco_s[par]
        s3_last = None
        s1_last = None
        for gi in range(NGRP):
            if DBG and gi > 0:
                break
            gI = s * NGRP + gi
            tc = GT * gi

            def xsl(t, h, tc=tc):
                sl, p = h % 2, h // 2
                return t[64 * sl:64 * sl + 64, SC * p + tc:SC * p + tc + GT]

            mm1 = []
            for h in range(NHL := 16):
                sl, p = h % 2, h // 2
                pb, pf = 64 * sl, 64 * p

                def f(e, h=h, pb=pb, pf=pf, xsl=xsl):
                    st, sk, sp = h < 2, h % 2 == 1, h >= 14
                    e.matmul(ps[2][pb:pb + 64, pf:pf + 64], xsl(xb, h),
                             g_w[pb:pb + 64, pf:pf + 64], start=st, stop=False,
                             skip_group_check=sk)
                    e.matmul(ps[2][pb:pb + 64, pf:pf + 64], xsl(xa, h),
                             c_negi[pb:pb + 64, :], start=False, stop=sp,
                             skip_group_check=sk)
                    e.matmul(ps[3][pb:pb + 64, pf:pf + 64], xsl(xbcov, h),
                             xsl(xb, h), start=st, stop=sp, skip_group_check=sk)
                    e.matmul(ps[4][pb:pb + 64, pf:pf + 64], xsl(xb, h),
                             xsl(xbcov, h), start=st, stop=sp, skip_group_check=sk)
                    return e.matmul(ps[5][pb:pb + 64, pf:pf + 64], xsl(xb, h),
                                    xsl(xccov, h), start=st, stop=sp, skip_group_check=sk)

                w = [x_ready[s], w_ready, last_z_evac] if h == 0 else []
                mm1.append(S.op("tensor", f, waits=w, inc=("sT", 1) if h == 15 else None))
            t_mm1 = mm1[-1]
            t_ltm = S.op("vector", lambda e: e.tensor_tensor(g_ltm[:, :], ps[3][:, :], c_mlt[:, :], op=ALU.mult),
                         waits=[t_mm1], inc=("sV", 1))
            t_lm = S.op("vector", lambda e: e.tensor_tensor(g_lm[:, :], ps[4][:, :], c_mlm[:, :], op=ALU.mult),
                        waits=[t_mm1], inc=("sV", 1))
            t_att = S.op("vector", lambda e: e.tensor_tensor(g_att[:, :], ps[5][:, :], c_mat[:, :], op=ALU.mult),
                         waits=[t_mm1], inc=("sV", 1))
            t_u = S.op("scalar", lambda e: e.activation(g_u[:, :], ps[2][:, :], AF.Copy),
                       waits=[t_mm1], inc=("sS", 1))
            yield

            mm2 = []
            for h in range(16):
                sl, p = h % 2, h // 2
                pb, pf = 64 * sl, 64 * p

                def f(e, h=h, pb=pb, pf=pf, xsl=xsl):
                    st, sk, sp = h < 2, h % 2 == 1, h >= 14
                    e.matmul(ps[4][pb:pb + 64, pf:pf + 64], xsl(xbcov, h),
                             xsl(xc, h), start=st, stop=sp, skip_group_check=sk)
                    e.matmul(ps[3][pb:pb + 64, pf:pf + 64], xsl(xbcov, h),
                             c_negi[pb:pb + 64, :], start=st, stop=sp, skip_group_check=sk)
                    return e.matmul(ps[5][pb:pb + 64, pf:pf + 64], g_lm[pb:pb + 64, pf:pf + 64],
                                    g_ltm[pb:pb + 64, pf:pf + 64], start=st, stop=sp,
                                    skip_group_check=sk)

                w = [t_ltm, t_lm, t_att, t_u] if h == 0 else []
                mm2.append(S.op("tensor", f, waits=w, inc=("sT", 1) if h == 15 else None))
            t_mm2 = mm2[-1]
            s1_last = t_mm2
            t_r1 = S.op("vector", lambda e: e.tensor_tensor(g_r1[:, :], ps[4][:, :], c_mr1[:, :], op=ALU.mult),
                        waits=[t_mm2], inc=("sV", 1))
            t_xbn = S.op("scalar", lambda e: e.activation(g_xbn[:, :], ps[3][:, :], AF.Copy),
                         waits=[t_mm2], inc=("sS", 1))
            t_lt2 = S.op("scalar", lambda e: e.activation(g_lt2[:, :], ps[5][:, :], AF.Copy),
                         waits=[t_mm2], inc=("sS", 1))
            yield

            mm3 = []
            for h in range(16):
                sl, p = h % 2, h // 2
                pb, pf = 64 * sl, 64 * p

                def f(e, h=h, pb=pb, pf=pf):
                    return e.matmul(ps[2][pb:pb + 64, pf:pf + 64], g_lt2[pb:pb + 64, pf:pf + 64],
                                    g_u[pb:pb + 64, pf:pf + 64], start=False, stop=False,
                                    skip_group_check=True)

                mm3.append(S.op("tensor", f, waits=[t_lt2, t_u] if h == 0 else [],
                                inc=("sT", 1) if h == 15 else None))
            t_t1n = S.op("scalar", lambda e: e.activation(g_t1n[:, :], ps[2][:, :], AF.Copy, scale=-1.0),
                         waits=[mm3[-1]], inc=("sS", 1))
            yield
            mm4 = []
            for h in range(16):
                sl, p = h % 2, h // 2
                pb, pf = 64 * sl, 64 * p

                def f(e, h=h, pb=pb, pf=pf):
                    return e.matmul(ps[2][pb:pb + 64, pf:pf + 64], g_ltm[pb:pb + 64, pf:pf + 64],
                                    g_t1n[pb:pb + 64, pf:pf + 64], start=False, stop=(h >= 14),
                                    skip_group_check=True)

                mm4.append(S.op("tensor", f, waits=[t_t1n] if h == 0 else [],
                                inc=("sT", 1) if h == 15 else None))
            t_z1 = S.op("scalar", lambda e: e.activation(g_z1[:, :], ps[2][:, :], AF.Copy),
                        waits=[mm4[-1]], inc=("sS", 1))
            yield

            mm5 = []
            for h in range(16):
                sl, p = h % 2, h // 2
                pb, pf = 64 * sl, 64 * p

                def f(e, h=h, pb=pb, pf=pf, xsl=xsl, gI=gI):
                    st, sk, sp = h < 2, h % 2 == 1, h >= 14
                    e.matmul(ps[2][pb:pb + 64, pf:pf + 64], g_w[pb:pb + 64, pf:pf + 64],
                             xsl(xc, h), start=st, stop=False, skip_group_check=sk)
                    e.matmul(ps[2][pb:pb + 64, pf:pf + 64], g_z1[pb:pb + 64, pf:pf + 64],
                             g_att[pb:pb + 64, pf:pf + 64], start=False, stop=False,
                             skip_group_check=sk)
                    e.matmul(ps[2][pb:pb + 64, pf:pf + 64], g_z1[pb:pb + 64, pf:pf + 64],
                             g_r1[pb:pb + 64, pf:pf + 64], start=False, stop=sp,
                             skip_group_check=sk)
                    return e.matmul(ps[6][pb:pb + 64, pf:pf + 64], g_xbn[pb:pb + 64, pf:pf + 64],
                                    g_z1[pb:pb + 64, pf:pf + 64],
                                    start=(gI == 0 and h < 2), stop=(h >= 14),
                                    skip_group_check=(sk or gI > 0))

                w = [t_z1, t_xbn, t_r1] if h == 0 else []
                mm5.append(S.op("tensor", f, waits=w, inc=("sT", 1) if h == 15 else None))
            t_mm5 = mm5[-1]
            s3_last = t_mm5
            zpar = gI % 2
            t_zt = S.op("vector",
                        (lambda zpar: (lambda e: e.tensor_copy(zst[zpar][:, :], ps[2][:, :])))(zpar),
                        waits=[t_mm5, zdma[-2] if len(zdma) >= 2 else None], inc=("sV", 1))
            last_z_evac = t_zt
            zdst = zallq[s // 4].ap().rearrange("r (sc p tau) -> r sc p tau", sc=4, p=NPAIR)[
                :, s % 4, :, GT * gi:GT * (gi + 1)]
            zsrc = zst[zpar][:, :].rearrange("r (p tau) -> r p tau", p=NPAIR)
            zdma.append(S.op("gpsimd", (lambda zd, zs: (lambda e: e.dma_start(out=zd, in_=zs)))(zdst, zsrc),
                             waits=[t_zt], inc=("dZ%d" % zpar, 16)))
            w_ready = S.op("vector", lambda e: e.tensor_tensor(g_w[:, :], c_w1[:, :], ps[6][:, :], op=ALU.add),
                           waits=[t_mm5], inc=("sV", 1))
            yield
        scan_s3_done[s] = s3_last
        scan_s1_done[s] = s1_last
        if s % 4 == 3:
            q = s // 4
            coll_ticks.append(S.op("gpsimd",
                (lambda q: (lambda e: e.collective_compute(
                    "AllGather", ALU.bypass,
                    replica_groups=[[0, 1], [2, 3], [4, 5], [6, 7]],
                    ins=[zallq[q].ap().opt()], outs=[zagoq[q].ap().opt()])))(q),
                waits=[zdma[-1]], inc=("sC", 1)))

    if DBG:
        if DBG_LEVEL >= 2:
            emit_ht_load(0)
        if DBG_LEVEL >= 3:
            emit_proj(0)
        if DBG_LEVEL >= 4:
            emit_scan(0)
        dumps = [wk_s[:, 15 * 1024:16 * 1024]]
        if DBG_LEVEL >= 2:
            dumps += [ht_s[0]]
        if DBG_LEVEL >= 3:
            dumps += [xbt_s[0], xct_s[0], xat_s, xbco_s, xcco_s]
        if DBG_LEVEL >= 4:
            dumps += [g_u, g_lm, g_ltm, g_att, g_r1, g_xbn, g_lt2, g_t1n, g_z1,
                      zst[0], g_w]
        off = 0
        t_dmp = None
        for dsrc in dumps:
            ncol = dsrc.shape[-1]
            nrow = dsrc.shape[0]
            dwaits = [tw_q, tw_k, tw_v, tw_i, t_wsb0, t_dmp]
            if DBG_LEVEL >= 2:
                dwaits.append(ht_ready[0])
            if DBG_LEVEL >= 3:
                dwaits.append(x_ready[0])
            if DBG_LEVEL >= 4:
                dwaits += [w_ready, last_z_evac, zdma[-1]]
            t_dmp = S.op("gpsimd",
                         (lambda dsrc, off, ncol, nrow: (lambda e: e.dma_start(
                             out=dbg_d.ap()[0:nrow, off:off + ncol], in_=dsrc)))(dsrc, off, ncol, nrow),
                         waits=dwaits, inc=("dZ", 16))
            off += ncol
        with nc.Block() as blk:
            @blk.tensor
            def _(eng):
                S.emit("tensor", eng, sems)

            @blk.vector
            def _(eng):
                S.emit("vector", eng, sems)

            @blk.scalar
            def _(eng):
                S.emit("scalar", eng, sems)

            @blk.sync
            def _(eng):
                S.emit("sync", eng, sems)

            @blk.gpsimd
            def _(eng):
                S.emit("gpsimd", eng, sems)
                eng.wait_ge(sems["dZ"], t_dmp[1])
        return nc

    emit_ht_load(0)
    for _ in gen_proj(0):
        pass
    for s in range(NSC):
        if s + 1 < NSC:
            emit_ht_load(s + 1)
        gs = gen_scan(s)
        gp = gen_proj(s + 1) if s + 1 < NSC else None
        k = 0
        while True:
            try:
                next(gs)
            except StopIteration:
                break
            if gp is not None:
                for _ in range(2 if k % 3 != 2 else 1):
                    try:
                        next(gp)
                    except StopIteration:
                        gp = None
                        break
            k += 1
        if gp is not None:
            for _ in gp:
                pass

    import os
    if os.environ.get("KSERIAL", "0") == "1":
        S.barrier_mode = True

    p1_free = S.op("vector", lambda e: e.tensor_copy(sc_s[0:1, 0:1], c_w1[0:1, 0:1]),
                   waits=[proj_done[NSC - 1]], inc=("sV", 1))
    p1_free2 = [proj_done[NSC - 1], scan_s3_done[NSC - 1], scan_s1_done[NSC - 1],
                w_ready, last_z_evac, x_ready[NSC - 1]]
    wsrc = wot_d.ap().rearrange("(k r) j -> r k j", r=128)
    wdst = wot_s.rearrange("r (k j) -> r k j", k=16)
    t_wot = None
    for i in range(4):
        t_wot = S.op("gpsimd", dma(wdst[:, 4 * i:4 * (i + 1), :], wsrc[:, 4 * i:4 * (i + 1), :]),
                     waits=[p1_free], inc=("dB", 16))

    t_wmax = S.op("vector", lambda e: e.memset(wmx_s[:, 0:1], 0.0), inc=("sV", 1))
    fw_prev = [None, None]
    for k in range(16):
        sp = k % 2
        t_l = S.op("sync", dma(fw_stg[sp][:, :], fwt_d.ap()[128 * k:128 * (k + 1), :]),
                   waits=[fw_prev[sp]], inc=("dF%d" % sp, 16))
        t_m = S.op("vector",
                   (lambda sp: (lambda e: e.reduce_max(wmx_s[:, 1:2], fw_stg[sp][:, :], axis=AXX,
                                                       apply_absolute_value=True)))(sp),
                   waits=[t_l, t_wmax], inc=("sV", 1))
        t_wmax = S.op("vector", lambda e: e.tensor_tensor(wmx_s[:, 0:1], wmx_s[:, 0:1], wmx_s[:, 1:2], op=ALU.max),
                      waits=[t_m], inc=("sV", 1))
        fw_prev[sp] = t_wmax

    t_mxacc = S.op("vector", lambda e: e.memset(mx_s[:, 0:1], 0.0), inc=("sV", 1))
    x1_ticks = []
    prev_sq = None
    hist = []

    def zview(bufs, row, X, tt):
        sc = X * (NSC // 2) + tt // 2
        v = bufs[sc // 4].ap().rearrange("r (c p tau) -> r c p tau", p=NPAIR, tau=SC)
        return v[row:row + 128, sc % 4, :, (tt % 2) * 128:(tt % 2) * 128 + 128]

    def zcoll(X, tt):
        return coll_ticks[(X * (NSC // 2) + tt // 2) // 4]

    import os as _os
    _G1 = _os.environ.get("KG", "")
    pend = {}

    def emit_part1(tt):
        par = tt % 2
        zpar4 = tt % 4
        h4 = hist[tt - 4] if tt >= 4 else None
        guard = h4["mm"] if h4 else None
        gw = [guard] if h4 else list(p1_free2)
        sem = "dC%d" % par
        t_l1 = S.op("sync", dma(ztl_s[zpar4][:, :].rearrange("r (p c) -> r p c", p=NPAIR), zview(zallq, 0, 0, tt)),
                    waits=gw, inc=(sem, 16))
        t_l2 = S.op("sync", dma(ztt_s[zpar4][:, :].rearrange("r (p c) -> r p c", p=NPAIR), zview(zallq, 0, 1, tt)),
                    waits=gw, inc=(sem, 16))
        t_p1 = S.op("sync", dma(ztp_s[zpar4][:, :].rearrange("r (p c) -> r p c", p=NPAIR), zview(zagoq, 128, 0, tt)),
                    waits=[zcoll(0, tt)] + gw, inc=(sem, 16))
        t_hr = S.op("sync", dma(hr_s[zpar4][:, :], hres_d.ap()[128 * tt:128 * (tt + 1), :]),
                    waits=gw, inc=(sem, 16))
        t_o1 = S.op("vector",
                    (lambda q4: (lambda e: e.tensor_scalar(ztl_s[q4][:, :], ztl_s[q4][:, :],
                                                           msel_s[:, 0:1], None, op0=ALU.mult)))(zpar4),
                    waits=[t_hr], inc=("sV", 1))
        t_own = S.op("vector",
                     (lambda q4: (lambda e: e.scalar_tensor_tensor(ztl_s[q4][:, :], ztt_s[q4][:, :],
                                                                   msel_s[:, 1:2], ztl_s[q4][:, :],
                                                                   op0=ALU.mult, op1=ALU.add)))(zpar4),
                     waits=[t_o1], inc=("sV", 1))
        t_q1 = S.op("vector",
                    (lambda q4: (lambda e: e.tensor_scalar(ztp_s[q4][:, :], ztp_s[q4][:, :],
                                                           msel_s[:, 2:3], None, op0=ALU.mult)))(zpar4),
                    waits=[t_hr], inc=("sV", 1))
        t_p2 = S.op("sync", dma(ztt_s[zpar4][:, :].rearrange("r (p c) -> r p c", p=NPAIR), zview(zagoq, 0, 1, tt)),
                    waits=[t_own, zcoll(1, tt)], inc=(sem, 16))
        t_par = S.op("vector",
                     (lambda q4: (lambda e: e.scalar_tensor_tensor(ztp_s[q4][:, :], ztt_s[q4][:, :],
                                                                   msel_s[:, 3:4], ztp_s[q4][:, :],
                                                                   op0=ALU.mult, op1=ALU.add)))(zpar4),
                     waits=[t_q1, t_p2], inc=("sV", 1))
        pend[tt] = (t_hr, t_own, t_par)

    def emit_part2(tt):
        nonlocal prev_sq, t_mxacc
        par = tt % 2
        zpar4 = tt % 4
        h2 = hist[tt - 2] if tt >= 2 else None
        t_hr, t_own, t_par = pend.pop(tt)
        mmt = []
        for ob in range(4):
            def f(e, ob=ob, par=par, tt=tt):
                bank = ps[par * 4 + ob]
                mm = None
                for jj in range(16):
                    lhs = (ztl_s[tt % 4] if jj < 8 else ztp_s[tt % 4])[:, 128 * (jj % 8):128 * (jj % 8) + 128]
                    mm = e.matmul(bank[:, :], lhs,
                                  wot_s[:, 2048 * jj + 512 * ob:2048 * jj + 512 * ob + 512],
                                  start=(jj == 0), stop=(jj == 15))
                return mm

            w = [t_own, t_par, t_wot, h2["ev"] if h2 else None]
            if tt == 0:
                w += [w_ready, last_z_evac]
            mmt.append(S.op("tensor", f, waits=w, inc=("sT", 1)))
        ev = []
        for ob in range(4):
            def f(e, ob=ob, par=par, q4=zpar4):
                return e.tensor_tensor(x1p_s[par][:, 512 * ob:512 * (ob + 1)],
                                       ps[par * 4 + ob][:, :],
                                       hr_s[q4][:, 512 * ob:512 * (ob + 1)], op=ALU.add)

            w = [mmt[ob], t_hr]
            if h2:
                w += [h2["x1"], h2["sqr"]]
            ev.append(S.op("vector", f, waits=w, inc=("sV", 1)))

        t_sqr = S.op("scalar",
                     (lambda par: (lambda e: e.activation(sq2_s[par][:, :], x1p_s[par][:, :], AF.Square,
                                                          accum_out=st_s[:, 1:2])))(par),
                     waits=ev + [prev_sq], inc=("sS", 1))
        t_st = S.op("vector",
                    (lambda par: (lambda e: e.reduce_sum(st_s[:, 0:1], x1p_s[par][:, :], axis=AXX)))(par),
                    waits=ev + [prev_sq], inc=("sV", 1))
        t_mn = S.op("vector", lambda e: e.tensor_scalar(st_s[:, 24:25], st_s[:, 0:1], 1.0 / D, None, op0=ALU.mult),
                    waits=[t_st], inc=("sV", 1))
        t_m2 = S.op("vector", lambda e: e.tensor_tensor(st_s[:, 29:30], st_s[:, 24:25], st_s[:, 24:25], op=ALU.mult),
                    waits=[t_mn], inc=("sV", 1))
        t_vr = S.op("vector", lambda e: e.scalar_tensor_tensor(st_s[:, 25:26], st_s[:, 1:2], 1.0 / D,
                                                               st_s[:, 29:30], op0=ALU.mult, op1=ALU.subtract),
                    waits=[t_m2, t_sqr], inc=("sV", 1))
        t_sq = S.op("scalar", lambda e: e.activation(st_s[:, 26:27], st_s[:, 25:26], AF.Sqrt, bias=eps_s[:, 0:1]),
                    waits=[t_vr, t_eps], inc=("sS", 1))
        prev_sq = t_sq
        t_rc = S.op("vector", lambda e: e.reciprocal(st_s[:, 27:28], st_s[:, 26:27]),
                    waits=[t_sq], inc=("sV", 1))
        t_nm = S.op("vector", lambda e: e.scalar_tensor_tensor(st_s[:, 28:29], st_s[:, 24:25], -1.0,
                                                               st_s[:, 27:28], op0=ALU.mult, op1=ALU.mult),
                    waits=[t_rc], inc=("sV", 1))
        t_x1 = S.op("vector",
                    (lambda par: (lambda e: e.tensor_scalar(x1b_s[par][:, :], x1p_s[par][:, :],
                                                            st_s[:, 27:28], st_s[:, 28:29],
                                                            op0=ALU.mult, op1=ALU.add)))(par),
                    waits=[t_nm, h2["xd"] if h2 else None], inc=("sV", 1))
        t_m = S.op("vector",
                   (lambda par: (lambda e: e.reduce_max(mx_s[:, 1:2], x1b_s[par][:, :], axis=AXX,
                                                        apply_absolute_value=True)))(par),
                   waits=[t_x1, t_mxacc], inc=("sV", 1))
        t_mxacc = S.op("vector", lambda e: e.tensor_tensor(mx_s[:, 0:1], mx_s[:, 0:1], mx_s[:, 1:2], op=ALU.max),
                       waits=[t_m], inc=("sV", 1))
        t_xd = S.op("gpsimd",
                    (lambda par, tt: (lambda e: e.dma_start(out=x1scr.ap()[128 * tt:128 * (tt + 1), :],
                                                            in_=x1b_s[par][:, :])))(par, tt),
                    waits=[t_x1], inc=("dX", 16))
        x1_ticks.append(t_xd)
        hist.append({"mm": mmt[-1], "ev": ev[-1], "x1": t_x1, "sqr": t_sqr, "xd": t_xd})

    LEAD = 2
    for tt in range(NTT):
        emit_part1(tt)
        if tt >= LEAD:
            emit_part2(tt - LEAD)
    for tt in range(NTT - LEAD, NTT):
        emit_part2(tt)

    if os.environ.get("KSERIAL", "0") == "3":
        S.barrier_mode = True
    t_mxd = S.op("gpsimd", lambda e: e.dma_start(out=mx_d.ap().rearrange("o c -> c o"), in_=mx_s[:, 0:1]),
                 waits=[t_mxacc], inc=("dM", 16))
    t_ar = S.op("gpsimd",
                lambda e: e.collective_compute("AllReduce", ALU.max, replica_groups=[list(range(8))],
                                               ins=[mx_d.ap().opt()], outs=[mxo_d.ap().opt()]),
                waits=[t_mxd], inc=("sC", 1))
    t_l = S.op("sync", dma(sc_s[0:8, 16:32], mxo_d.ap().rearrange("o (a b) -> (o a) b", b=16)),
               waits=[t_ar], inc=("dA", 16))
    t_r = S.op("vector", lambda e: e.reduce_max(sc_s[0:8, 32:33], sc_s[0:8, 16:32], axis=AXX),
               waits=[t_l], inc=("sV", 1))
    t_d = S.op("gpsimd", lambda e: e.dma_start(out=mx_d.ap()[0:1, 0:8].rearrange("o c -> c o"),
                                               in_=sc_s[0:8, 32:33]),
               waits=[t_r], inc=("dM", 16))
    t_l2 = S.op("sync", dma(sc_s[0:1, 33:41], mx_d.ap()[0:1, 0:8]), waits=[t_d], inc=("dA", 16))
    t_xs = S.op("vector", lambda e: e.reduce_max(sc_s[0:1, 8:9], sc_s[0:1, 33:41], axis=AXX),
                waits=[t_l2], inc=("sV", 1))
    t_xs = S.op("vector", lambda e: e.tensor_scalar(sc_s[0:1, 8:9], sc_s[0:1, 8:9], 1e-6, None, op0=ALU.max),
                waits=[t_xs], inc=("sV", 1))
    t_wd = S.op("gpsimd", lambda e: e.dma_start(out=mx_d.ap().rearrange("o c -> c o"),
                                                in_=wmx_s[:, 0:1]),
                waits=[t_wmax, t_l, t_l2], inc=("dM", 16))
    t_wl = S.op("sync", dma(sc_s[0:8, 16:32], mx_d.ap().rearrange("o (a b) -> (o a) b", b=16)),
                waits=[t_wd, t_r], inc=("dA", 16))
    t_wr = S.op("vector", lambda e: e.reduce_max(sc_s[0:8, 32:33], sc_s[0:8, 16:32], axis=AXX),
                waits=[t_wl, t_xs], inc=("sV", 1))
    t_wd2 = S.op("gpsimd", lambda e: e.dma_start(out=mx_d.ap()[0:1, 0:8].rearrange("o c -> c o"),
                                                 in_=sc_s[0:8, 32:33]),
                 waits=[t_wr, t_l2], inc=("dM", 16))
    t_wl2 = S.op("sync", dma(sc_s[0:1, 33:41], mx_d.ap()[0:1, 0:8]), waits=[t_wd2], inc=("dA", 16))
    t_ws = S.op("vector", lambda e: e.reduce_max(sc_s[0:1, 9:10], sc_s[0:1, 33:41], axis=AXX),
                waits=[t_wl2], inc=("sV", 1))
    t_ws = S.op("vector", lambda e: e.tensor_scalar(sc_s[0:1, 9:10], sc_s[0:1, 9:10], 1e-6, None, op0=ALU.max),
                waits=[t_ws], inc=("sV", 1))
    t_rws = S.op("vector", lambda e: e.reciprocal(sc_s[0:1, 11:12], sc_s[0:1, 9:10]),
                 waits=[t_ws], inc=("sV", 1))
    t_bcwm = S.op("tensor", lambda e: e.matmul(ps[7][:, 4:5], onesv_s[0:1, 0:128],
                                               sc_s[0:1, 11:12], start=True, stop=True),
                  waits=[t_rws, t_const], inc=("sT", 1))
    t_bcw = S.op("vector", lambda e: e.tensor_copy(sc_s[:, 21:22], ps[7][:, 4:5]),
                 waits=[t_bcwm], inc=("sV", 1))
    t_rxs = S.op("vector", lambda e: e.reciprocal(sc_s[0:1, 10:11], sc_s[0:1, 8:9]),
                 waits=[t_xs], inc=("sV", 1))
    t_xw = S.op("vector", lambda e: e.tensor_tensor(sc_s[0:1, 12:13], sc_s[0:1, 8:9],
                                                    sc_s[0:1, 9:10], op=ALU.mult),
                waits=[t_xs, t_ws], inc=("sV", 1))
    t_bcm = S.op("tensor", lambda e: e.matmul(ps[7][:, 0:3], onesv_s[0:1, 0:128],
                                              sc_s[0:1, 10:13], start=True, stop=True),
                 waits=[t_rxs, t_xw, t_const], inc=("sT", 1))
    t_bc = S.op("vector", lambda e: e.tensor_copy(sc_s[:, 24:27], ps[7][:, 0:3]),
                waits=[t_bcm], inc=("sV", 1))

    if os.environ.get("KSERIAL", "0") == "4":
        S.barrier_mode = True
    wq_ticks = []
    for k in range(16):
        sp = k % 2
        t_lw = S.op("sync", dma(fw_stg[sp][:, :], fwt_d.ap()[128 * k:128 * (k + 1), :]),
                    waits=[wq_ticks[-2] if len(wq_ticks) >= 2 else t_bcw], inc=("dF%d" % sp, 16))
        t_q = S.op("scalar",
                   (lambda sp, k: (lambda e: e.activation(wq8_s[:, 2048 * k:2048 * (k + 1)],
                                                          fw_stg[sp][:, :], AF.Copy,
                                                          scale=sc_s[:, 21:22])))(sp, k),
                   waits=[t_lw, t_bcw], inc=("sS", 1))
        wq_ticks.append(t_q)
    t_wq = wq_ticks[-1]

    xq_ticks = []
    for k in range(16):
        xpar = k % 4
        t_t = S.op("sync",
                   (lambda k, xpar: (lambda e: e.dma_start_transpose(
                       x1t[:, 2048 * xpar:2048 * (xpar + 1)], x1scr.ap()[:, 128 * k:128 * (k + 1)])))(k, xpar),
                   waits=[x1_ticks[-1], xq_ticks[-4] if len(xq_ticks) >= 4 else None],
                   inc=("dT%d" % (xpar % 2), 16))
        eng = "scalar" if k % 2 == 0 else "vector"
        if eng == "scalar":
            t_q = S.op(eng,
                       (lambda k, xpar: (lambda e: e.activation(xq8_s[:, 2048 * k:2048 * (k + 1)],
                                                                x1t[:, 2048 * xpar:2048 * (xpar + 1)], AF.Copy,
                                                                scale=sc_s[:, 24:25])))(k, xpar),
                       waits=[t_t, t_bc], inc=("sS", 1))
        else:
            t_q = S.op(eng,
                       (lambda k, xpar: (lambda e: e.tensor_scalar(xq8_s[:, 2048 * k:2048 * (k + 1)],
                                                                   x1t[:, 2048 * xpar:2048 * (xpar + 1)],
                                                                   sc_s[:, 24:25], None, op0=ALU.mult)))(k, xpar),
                       waits=[t_t, t_bc], inc=("sV", 1))
        xq_ticks.append(t_q)
    t_xq = xq_ticks[-1]
    t_xq2 = xq_ticks[-2]

    out_ticks = []
    prev_sq4 = None
    for tt in range(NTT):
        par = tt % 2
        guard = out_ticks[-2] if len(out_ticks) >= 2 else None
        t_x1r = S.op("sync", dma(x1r_s[par][:, :], x1scr.ap()[128 * tt:128 * (tt + 1), :]),
                     waits=[x1_ticks[-1], guard], inc=("dR%d" % par, 16))
        mmt = []
        for ob in range(4):
            def f(e, ob=ob, par=par, tt=tt):
                bank = ps[par * 4 + ob]
                mm = None
                for k in range(16):
                    mm = e.matmul(bank[:, :],
                                  xq8_s[:, 2048 * k + 128 * tt:2048 * k + 128 * tt + 128],
                                  wq8_s[:, 2048 * k + 512 * ob:2048 * k + 512 * ob + 512],
                                  start=(k == 0), stop=(k == 15))
                return mm

            w = [t_wq, t_xq, t_xq2, guard]
            if tt == 0:
                w.append(t_bcm)
            mmt.append(S.op("tensor", f, waits=w, inc=("sT", 1)))
        ev = []
        for ob in range(4):
            def f1(e, ob=ob, par=par):
                return e.tensor_scalar(y_s[par][:, 512 * ob:512 * (ob + 1)], ps[par * 4 + ob][:, :],
                                       sc_s[:, 26:27], None, op0=ALU.mult)

            t_y = S.op("vector", f1, waits=[mmt[ob], t_bc], inc=("sV", 1))

            def f2(e, ob=ob, par=par):
                return e.tensor_tensor(y_s[par][:, 512 * ob:512 * (ob + 1)],
                                       y_s[par][:, 512 * ob:512 * (ob + 1)],
                                       fb_s[:, 512 * ob:512 * (ob + 1)], op=ALU.add)

            t_y2 = S.op("vector", f2, waits=[t_y, t_const], inc=("sV", 1))

            def f3(e, ob=ob, par=par):
                return e.tensor_tensor(x2p_s[par][:, 512 * ob:512 * (ob + 1)],
                                       y_s[par][:, 512 * ob:512 * (ob + 1)],
                                       x1r_s[par][:, 512 * ob:512 * (ob + 1)], op=ALU.add)

            ev.append(S.op("vector", f3, waits=[t_y2, t_x1r], inc=("sV", 1)))

        t_sqr = S.op("scalar",
                     (lambda par: (lambda e: e.activation(sq4_s[par][:, :], x2p_s[par][:, :], AF.Square,
                                                          accum_out=st4_s[:, 1:2])))(par),
                     waits=ev + [prev_sq4], inc=("sS", 1))
        t_st = S.op("vector",
                    (lambda par: (lambda e: e.reduce_sum(st4_s[:, 0:1], x2p_s[par][:, :], axis=AXX)))(par),
                    waits=ev + [prev_sq4], inc=("sV", 1))
        t_mn = S.op("vector", lambda e: e.tensor_scalar(st4_s[:, 24:25], st4_s[:, 0:1], 1.0 / D, None, op0=ALU.mult),
                    waits=[t_st], inc=("sV", 1))
        t_m2 = S.op("vector", lambda e: e.tensor_tensor(st4_s[:, 29:30], st4_s[:, 24:25], st4_s[:, 24:25], op=ALU.mult),
                    waits=[t_mn], inc=("sV", 1))
        t_vr = S.op("vector", lambda e: e.scalar_tensor_tensor(st4_s[:, 25:26], st4_s[:, 1:2], 1.0 / D,
                                                               st4_s[:, 29:30], op0=ALU.mult, op1=ALU.subtract),
                    waits=[t_m2, t_sqr], inc=("sV", 1))
        t_sq = S.op("scalar", lambda e: e.activation(st4_s[:, 26:27], st4_s[:, 25:26], AF.Sqrt, bias=eps_s[:, 0:1]),
                    waits=[t_vr, t_eps], inc=("sS", 1))
        prev_sq4 = t_sq
        t_rc = S.op("vector", lambda e: e.reciprocal(st4_s[:, 27:28], st4_s[:, 26:27]),
                    waits=[t_sq], inc=("sV", 1))
        t_nm = S.op("vector", lambda e: e.scalar_tensor_tensor(st4_s[:, 28:29], st4_s[:, 24:25], -1.0,
                                                               st4_s[:, 27:28], op0=ALU.mult, op1=ALU.mult),
                    waits=[t_rc], inc=("sV", 1))
        t_o = S.op("vector",
                   (lambda par: (lambda e: e.tensor_scalar(o_s[par][:, :], x2p_s[par][:, :],
                                                           st4_s[:, 27:28], st4_s[:, 28:29],
                                                           op0=ALU.mult, op1=ALU.add)))(par),
                   waits=[t_nm], inc=("sV", 1))
        t_od = S.op("gpsimd",
                    (lambda par, tt: (lambda e: e.dma_start(out=out_d.ap()[128 * tt:128 * (tt + 1), :],
                                                            in_=o_s[par][:, :])))(par, tt),
                    waits=[t_o], inc=("dO", 16))
        out_ticks.append(t_od)

    final = out_ticks[-1]

    if DBG2:
        zall_o = nc.dram_tensor("zall_o", [128, NSC * 2048], BF16, kind="ExternalOutput")
        x1_o = nc.dram_tensor("x1_o", [TOK, D], BF16, kind="ExternalOutput")
        scl_o = nc.dram_tensor("scl_o", [128, 64], F32, kind="ExternalOutput")
        zago_o = nc.dram_tensor("zago_o", [256, NSC * 2048], BF16, kind="ExternalOutput")
        t_d1 = S.op("gpsimd", lambda e: e.dma_start(out=zall_o.ap(), in_=zall.ap()),
                    waits=[zdma[-1], out_ticks[-1]], inc=("dZ", 16))
        t_d2 = S.op("gpsimd", lambda e: e.dma_start(out=x1_o.ap(), in_=x1scr.ap()),
                    waits=[t_d1], inc=("dZ", 16))
        t_d3 = S.op("gpsimd", lambda e: e.dma_start(out=zago_o.ap(), in_=zago.ap()),
                    waits=[t_d2], inc=("dZ", 16))
        t_d4 = S.op("gpsimd", lambda e: e.dma_start(out=scl_o.ap(), in_=sc_s[:, 0:64].bitcast(BF16).bitcast(F32)),
                    waits=[t_bc, t_d3], inc=("dZ", 16))
        final = t_d4
        fsem = "dZ"
    else:
        fsem = "dO"

    rec = {} if os.environ.get("KREC", "0") == "1" else None
    with nc.Block() as blk:
        @blk.tensor
        def _(eng):
            S.emit("tensor", eng, sems, nc, rec)

        @blk.vector
        def _(eng):
            S.emit("vector", eng, sems, nc, rec)

        @blk.scalar
        def _(eng):
            S.emit("scalar", eng, sems, nc, rec)

        @blk.sync
        def _(eng):
            S.emit("sync", eng, sems, nc, rec)

        @blk.gpsimd
        def _(eng):
            S.emit("gpsimd", eng, sems, nc, rec)
            eng.wait_ge(sems[fsem], final[1])
    nc._sched = S
    nc._rec = rec

    return nc


def _consts():
    negi = np.tile(-np.eye(64, dtype=np.float32), (2, 1)).astype(BF)
    ci = np.arange(GT) // CS
    up = (ci[None, :] > ci[:, None]).astype(np.float32)
    tri = ((ci[:, None] == ci[None, :]) &
           (np.arange(GT)[:, None] <= np.arange(GT)[None, :])).astype(np.float32)

    def tile(m):
        o = np.zeros((128, 512), np.float32)
        for sl in range(2):
            for p in range(8):
                o[64 * sl:64 * sl + 64, 64 * p:64 * p + 64] = m
        return o.astype(BF)

    mlt, mlm, matt, mr1 = tile(up), tile(up.T), tile(-tri), tile(-up)
    tokf = np.tile((1.0 / ((np.arange(SC) % CS) + 1.0) / HF).astype(np.float32)[None, :], (16, 1))
    sel = np.zeros((16, 1024), np.float32)
    for h in range(16):
        p, sl = h // 2, h % 2
        sel[h, 128 * p + 64 * sl:128 * p + 64 * sl + 64] = 1.0
    return negi, mlt, mlm, matt, mr1, np.ascontiguousarray(tokf), np.ascontiguousarray(sel), np.ones((1, 128), np.float32)


def kernel(L0_attn_hidden_states, L0_attn_Wq, L0_attn_Wk, L0_attn_Wv, L0_attn_Wo,
           L0_attn_W_ilr, L0_attn_b_ilr, L0_attn_W1, L0_ffn_weight, L0_ffn_bias):
    hid = np.ascontiguousarray(np.asarray(L0_attn_hidden_states, np.float32))
    Wq = np.asarray(L0_attn_Wq, np.float32)
    Wk = np.asarray(L0_attn_Wk, np.float32)
    Wv = np.asarray(L0_attn_Wv, np.float32)
    Wo = np.asarray(L0_attn_Wo, np.float32)
    Wilr = np.asarray(L0_attn_W_ilr, np.float32)
    bilr = np.asarray(L0_attn_b_ilr, np.float32)
    W1 = np.asarray(L0_attn_W1, np.float32)
    fw = np.asarray(L0_ffn_weight)
    fb = np.asarray(L0_ffn_bias)

    negi, mlt, mlm, matt, mr1, tokf, sel, ones1 = _consts()
    fwt = np.ascontiguousarray(fw.T)
    fbias = np.ascontiguousarray(np.tile(np.asarray(fb)[None, :], (128, 1)))

    in_maps = []
    for c in range(8):
        b, hh = c // 2, c % 2
        Hs = slice(1024 * hh, 1024 * (hh + 1))
        w1i = np.zeros((128, 512), np.float32)
        for hl in range(16):
            p, sl = hl // 2, hl % 2
            w1i[64 * sl:64 * sl + 64, 64 * p:64 * p + 64] = W1[16 * hh + hl]
        wot = np.ascontiguousarray(np.concatenate(
            [Wo[:, Hs].T, Wo[:, 1024 * (1 - hh):1024 * (2 - hh)].T], axis=0))
        msel = np.zeros((128, 4), np.float32)
        msel[:, 0] = 1.0 if hh == 0 else 0.0
        msel[:, 1] = 1.0 if hh == 1 else 0.0
        msel[:, 2] = 1.0 if hh == 0 else 0.0
        msel[:, 3] = 1.0 if hh == 1 else 0.0
        in_maps.append({
            "hT": np.ascontiguousarray(hid[b].T),
            "hres": np.ascontiguousarray(hid[b][TOK * hh:TOK * (hh + 1), :]),
            "wqt": np.ascontiguousarray(Wq[Hs, :].T),
            "wkt": np.ascontiguousarray(Wk[Hs, :].T),
            "wvt": np.ascontiguousarray(Wv[Hs, :].T),
            "wilrt": np.ascontiguousarray(Wilr[16 * hh:16 * hh + 16, :].T),
            "bilr": np.ascontiguousarray(bilr[16 * hh:16 * hh + 16, None]),
            "w1init": w1i, "wot": wot, "fwt": fwt, "fbias": fbias,
            "negi": negi, "mlt": mlt, "mlm": mlm, "matt": matt, "mr1": mr1,
            "tokf": tokf, "sel": sel, "ones1": ones1, "msel": msel,
        })

    nc = build_nc()
    res = run_bass_kernel_spmd(nc, in_maps, core_ids=list(range(8)))
    out = np.zeros((B, L, D), np.float32)
    for c in range(8):
        b, hh = c // 2, c % 2
        out[b, TOK * hh:TOK * (hh + 1), :] = res.results[c]["out"]
    return out


# version 10 (final)
# speedup vs baseline: 1.0834x; 1.0016x over previous
import numpy as np
import ml_dtypes
import concourse.bass as bass
import concourse.mybir as mybir
from concourse.bass_utils import run_bass_kernel_spmd

F32 = mybir.dt.float32
BF16 = mybir.dt.bfloat16
FP8 = mybir.dt.float8e4
AF = mybir.ActivationFunctionType
ALU = mybir.AluOpType
AXX = mybir.AxisListType.X
BF = ml_dtypes.bfloat16

DBG = False
DBG_LEVEL = 4
PROJ_SUB = 3
PROJ_NCH = 99
DBG2 = False
B, L, D = 4, 4096, 2048
NH, CS, HF = 32, 16, 64
NPAIR = 8
SC = 256
NSC = L // SC
G = 4
GT = G * CS
NGRP = SC // GT
TOK = L // 2
NTT = TOK // 128
EPS = 1e-5


class Sched:

    def __init__(self):
        self.events = []
        self.count = {}
        self.last_wait = {}
        self.barrier_mode = False

    def op(self, eng, fn, waits=(), inc=None):
        if self.barrier_mode:
            waits = list(waits) + [(s, v) for s, v in self.count.items()]
        w = {}
        for t in waits:
            if t is None:
                continue
            sem, val = t
            if self.last_wait.get((eng, sem), -1) >= val:
                continue
            if w.get(sem, -1) < val:
                w[sem] = val
        for sem, val in w.items():
            self.last_wait[(eng, sem)] = val
        tick = None
        if inc is not None:
            sem, n = inc
            self.count[sem] = self.count.get(sem, 0) + n
            tick = (sem, self.count[sem])
        self.events.append((eng, w, fn, inc))
        return tick

    def emit(self, eng_name, eng, sems, nc=None, rec=None):
        for gid, (e, waits, fn, inc) in enumerate(self.events):
            if e != eng_name:
                continue
            for sem, val in waits.items():
                eng.wait_ge(sems[sem], val)
            i0 = nc.n_instructions() if rec is not None else 0
            instr = fn(eng)
            if inc is not None:
                sem, n = inc
                assert instr is not None, f"no instr for inc {inc}"
                instr.then_inc(sems[sem], n)
            if rec is not None:
                rec[gid] = (i0, nc.n_instructions())


def build_nc():
    nc = bass.Bass("TRN2", target_bir_lowering=False, detect_race_conditions=False)

    hT_d = nc.dram_tensor("hT", [D, L], F32, kind="ExternalInput")
    hres_d = nc.dram_tensor("hres", [TOK, D], F32, kind="ExternalInput")
    wqt_d = nc.dram_tensor("wqt", [D, 1024], F32, kind="ExternalInput")
    wkt_d = nc.dram_tensor("wkt", [D, 1024], F32, kind="ExternalInput")
    wvt_d = nc.dram_tensor("wvt", [D, 1024], F32, kind="ExternalInput")
    wilrt_d = nc.dram_tensor("wilrt", [D, 16], F32, kind="ExternalInput")
    bilr_d = nc.dram_tensor("bilr", [16, 1], F32, kind="ExternalInput")
    w1init_d = nc.dram_tensor("w1init", [128, 512], F32, kind="ExternalInput")
    wot_d = nc.dram_tensor("wot", [D, D], F32, kind="ExternalInput")
    fwt_d = nc.dram_tensor("fwt", [D, D], BF16, kind="ExternalInput")
    fbias_d = nc.dram_tensor("fbias", [128, D], BF16, kind="ExternalInput")
    negi_d = nc.dram_tensor("negi", [128, 64], BF16, kind="ExternalInput")
    mlt_d = nc.dram_tensor("mlt", [128, 512], BF16, kind="ExternalInput")
    mlm_d = nc.dram_tensor("mlm", [128, 512], BF16, kind="ExternalInput")
    mat_d = nc.dram_tensor("matt", [128, 512], BF16, kind="ExternalInput")
    mr1_d = nc.dram_tensor("mr1", [128, 512], BF16, kind="ExternalInput")
    tokf_d = nc.dram_tensor("tokf", [16, SC], F32, kind="ExternalInput")
    sel_d = nc.dram_tensor("sel", [16, 1024], F32, kind="ExternalInput")
    ones_d = nc.dram_tensor("ones1", [1, 128], F32, kind="ExternalInput")
    msel_d = nc.dram_tensor("msel", [128, 4], F32, kind="ExternalInput")
    out_d = nc.dram_tensor("out", [TOK, D], F32, kind="ExternalOutput")

    zallq = [nc.dram_tensor(f"zallq{i}", [128, 4 * 2048], BF16, kind="Internal") for i in range(4)]
    zagoq = [nc.dram_tensor(f"zagoq{i}", [256, 4 * 2048], BF16, kind="Internal") for i in range(4)]
    x1scr = nc.dram_tensor("x1scr", [TOK, D], BF16, kind="Internal")
    mx_d = nc.dram_tensor("mxin", [1, 128], F32, kind="Internal")
    dbg_d = nc.dram_tensor("dbg", [128, 40960], BF16, kind="Internal")
    mxo_d = nc.dram_tensor("mxout", [1, 128], F32, kind="Internal")

    ARENA = 96 * 1024 - 128
    keep = []
    a_cm = nc.sbuf_tensor("arena", [128, ARENA], BF16)
    arena = a_cm.__enter__()
    keep.append(a_cm)

    class Carver:
        def __init__(self, base):
            self.off = base

        def take(self, elems, dt=BF16):
            slots = elems * 2 if dt == F32 else ((elems + 1) // 2 if dt == FP8 else elems)
            o = (self.off + 31) & ~31
            self.off = o + slots
            assert self.off <= ARENA, f"arena overflow at {self.off}"
            v = arena[:, o:o + slots]
            if dt == F32:
                return v.bitcast(F32)
            if dt == FP8:
                return v.bitcast(FP8)
            return v

    g = Carver(0)
    c_negi = g.take(64)
    c_mlt = g.take(512)
    c_mlm = g.take(512)
    c_mat = g.take(512)
    c_mr1 = g.take(512)
    c_tokf = g.take(SC, F32)
    c_sel = g.take(1024, F32)
    c_w1 = g.take(512, F32)
    GB = g.off

    tail = Carver(GB)
    fw_stg = [tail.take(2048) for _ in range(2)]
    mx_s = tail.take(8, F32)
    wmx_s = tail.take(8, F32)
    st_s = tail.take(64, F32)
    st4_s = tail.take(64, F32)
    sc_s = tail.take(64, F32)
    fb_s = tail.take(2048)
    bl_s = tail.take(8, F32)
    onesv_s = tail.take(128, F32)
    msel_s = tail.take(4, F32)
    eps_s = tail.take(8, F32)
    OV = tail.off

    p1 = Carver(OV)
    wq_s = p1.take(16 * 1024)
    wk_s = p1.take(16 * 1024)
    wv_s = p1.take(16 * 1024)
    wilr_s = p1.take(16 * 16)
    ht_s = [p1.take(16 * SC) for _ in range(2)]
    xbt_s = [p1.take(NPAIR * SC) for _ in range(2)]
    xct_s = [p1.take(NPAIR * SC) for _ in range(2)]
    xat_s = [p1.take(NPAIR * SC) for _ in range(2)]
    xbco_s = [p1.take(NPAIR * SC) for _ in range(2)]
    xcco_s = [p1.take(NPAIR * SC) for _ in range(2)]
    co_s = p1.take(SC, F32)
    cox_s = p1.take(SC, F32)
    g_u = p1.take(512)
    g_lm = p1.take(512)
    g_ltm = p1.take(512)
    g_att = p1.take(512)
    g_r1 = p1.take(512)
    g_xbn = p1.take(512)
    g_lt2 = p1.take(512)
    g_t1n = p1.take(512)
    g_z1 = p1.take(512)
    g_w = p1.take(512)
    zst = [p1.take(512) for _ in range(2)]
    P1E = p1.off

    p2 = Carver(OV)
    wot_s = p2.take(16 * 2048)
    ztl_s = [p2.take(1024) for _ in range(4)]
    ztp_s = [p2.take(1024) for _ in range(4)]
    ztt_s = [p2.take(1024) for _ in range(4)]
    hr_s = [p2.take(2048, F32) for _ in range(4)]
    x1p_s = [p2.take(2048, F32) for _ in range(2)]
    x1b_s = [p2.take(2048) for _ in range(2)]
    sq2_s = [p2.take(2048, F32) for _ in range(2)]
    P2E = p2.off

    p4 = Carver(OV)
    wq8_s = p4.take(16 * 2048, FP8)
    xq8_s = p4.take(16 * 2048, FP8)
    y_s = [p4.take(2048) for _ in range(2)]
    x1r_s = [p4.take(2048) for _ in range(2)]
    x2p_s = [p4.take(2048, F32) for _ in range(2)]
    o_s = [p4.take(2048, F32) for _ in range(2)]
    sq4_s = [p4.take(2048, F32) for _ in range(2)]
    P4E = p4.off
    assert max(P1E, P2E, P4E) <= ARENA, (P1E, P2E, P4E)

    x1t_cm = nc.sbuf_tensor("x1t", [128, 4 * 2048], BF16, is_transpose_dest=True)
    x1t = x1t_cm.__enter__()
    keep.append(x1t_cm)

    ps = []
    for i in range(8):
        cm = nc.psum_tensor(f"ps{i}", [128, 512], F32)
        ps.append(cm.__enter__())
        keep.append(cm)

    sems = {}
    for s in ["dA", "dB", "dC0", "dC1", "dZ0", "dZ1", "dX", "dM", "dO",
              "dG", "dH0", "dH1", "dF0", "dF1", "dT0", "dT1", "dR0", "dR1",
              "sT", "sV", "sS", "sC"]:
        cm = nc.semaphore(s)
        sems[s] = cm.__enter__()
        keep.append(cm)

    S = Sched()
    import os
    if os.environ.get("KSERIAL", "0") in ("2", "full"):
        S.barrier_mode = True

    def dma(dst, src):
        return lambda e: e.dma_start(out=dst, in_=src)

    t_c = None
    for dst, src in [(c_negi[:, :], negi_d.ap()), (c_mlt[:, :], mlt_d.ap()),
                     (c_mlm[:, :], mlm_d.ap()), (c_mat[:, :], mat_d.ap()),
                     (c_mr1[:, :], mr1_d.ap()), (c_tokf[0:16, :], tokf_d.ap()),
                     (c_sel[0:16, :], sel_d.ap()), (c_w1[:, :], w1init_d.ap()),
                     (bl_s[0:16, 0:1], bilr_d.ap()), (onesv_s[0:1, :], ones_d.ap()),
                     (msel_s[:, :], msel_d.ap()), (fb_s[:, :], fbias_d.ap())]:
        t_c = S.op("sync", dma(dst, src), inc=("dA", 16))
    t_const = t_c

    def load_weight(dst_s, src_d, ncols, nsplit=2):
        t = None
        src = src_d.ap().rearrange("(k r) j -> r k j", r=128)
        dst = dst_s.rearrange("r (k j) -> r k j", k=16)
        step = 16 // nsplit
        for i in range(nsplit):
            t = S.op("gpsimd", dma(dst[:, step * i:step * (i + 1), :],
                                   src[:, step * i:step * (i + 1), :]),
                     inc=("dG", 16))
        return t

    load_weight(wq_s, wqt_d, 1024)
    load_weight(wk_s, wkt_d, 1024)
    load_weight(wv_s, wvt_d, 1024)
    tw_all = load_weight(wilr_s, wilrt_d, 16, nsplit=1)
    tw_q = tw_k = tw_v = tw_i = tw_all
    t_wsb0 = S.op("vector", lambda e: e.tensor_copy(g_w[:, :], c_w1[:, :]),
                  waits=[t_const], inc=("sV", 1))
    t_eps = S.op("vector", lambda e: e.memset(eps_s[:, 0:1], EPS), inc=("sV", 1))

    ht_ready = [None] * (NSC + 1)
    proj_done = [None] * (NSC + 1)
    x_ready = [None] * (NSC + 1)
    scan_s3_done = [None] * (NSC + 1)
    scan_s1_done = [None] * (NSC + 1)
    w_ready = t_wsb0
    last_z_evac = None
    zdma = []
    coll_ticks = []

    def emit_ht_load(s):
        par = s % 2
        sem = "dH%d" % par
        src = hT_d.ap().rearrange("(k r) t -> r k t", r=128)[:, :, SC * s:SC * (s + 1)]
        dst = ht_s[par].rearrange("r (k t) -> r k t", k=16)
        t = None
        for i in range(2):
            t = S.op("gpsimd", dma(dst[:, 8 * i:8 * (i + 1), :], src[:, 8 * i:8 * (i + 1), :]),
                     waits=[proj_done[s - 2] if s >= 2 else tw_all],
                     inc=(sem, 16))
        ht_ready[s] = t

    def gen_proj(s):
        par = s % 2
        deps = [ht_ready[s], tw_q, tw_k, tw_v, tw_i, t_const]
        evq = []
        ev_all = []
        mm_last = None
        for wi, (w_s, dst2) in enumerate([(wk_s, xbt_s[par]), (wq_s, xct_s[par]), (wv_s, None)]):
            for p in range(NPAIR):
                ci = wi * NPAIR + p
                if DBG and ci >= PROJ_NCH:
                    continue
                bank = ps[ci % 2]

                def mmchain(e, w_s=w_s, p=p, bank=bank, par=par):
                    mm = None
                    for k in range(16):
                        mm = e.matmul(bank[:, 0:SC],
                                      w_s[:, 1024 * k + 128 * p:1024 * k + 128 * p + 128],
                                      ht_s[par][:, SC * k:SC * (k + 1)],
                                      start=(k == 0), stop=(k == 15))
                    return mm

                w = list(deps) + [evq[-2] if len(evq) >= 2 else None]
                if p == 0:
                    w.append(scan_s3_done[s - 2] if s >= 2 else None)
                    w.append(scan_s1_done[s - 2] if s >= 2 else None)
                t_mm = S.op("tensor", mmchain, waits=w, inc=("sT", 1))
                mm_last = t_mm
                dst = dst2[:, SC * p:SC * (p + 1)] if dst2 is not None else xat_s[par][:, SC * p:SC * (p + 1)]
                t_ev = S.op("vector",
                            (lambda bank, dst: (lambda e: e.tensor_copy(dst, bank)))(bank[:, 0:SC], dst),
                            waits=[t_mm], inc=("sV", 1))
                evq.append(t_ev)
                ev_all.append(t_ev)
                yield
        proj_mm_last = mm_last
        if DBG and PROJ_SUB < 2:
            x_ready[s] = evq[-1]
            proj_done[s] = proj_mm_last
            return

        def ilrchain(e, par=par):
            mm = None
            for k in range(16):
                mm = e.matmul(ps[7][0:16, 0:SC], wilr_s[:, 16 * k:16 * (k + 1)],
                              ht_s[par][:, SC * k:SC * (k + 1)], start=(k == 0), stop=(k == 15))
            return mm

        t_ilr = S.op("tensor", ilrchain, waits=deps + [x_ready[s - 1] if s >= 1 else None],
                     inc=("sT", 1))
        t_sig = S.op("scalar",
                     lambda e: e.activation(co_s[0:16, :], ps[7][0:16, 0:SC], AF.Sigmoid,
                                            bias=bl_s[0:16, 0:1]),
                     waits=[t_ilr, x_ready[s - 1] if s >= 1 else None], inc=("sS", 1))
        t_co = S.op("vector",
                    lambda e: e.tensor_tensor(co_s[0:16, :], co_s[0:16, :], c_tokf[0:16, :], op=ALU.mult),
                    waits=[t_sig, t_const], inc=("sV", 1))
        last = t_co
        if DBG and PROJ_SUB < 3:
            x_ready[s] = t_co
            proj_done[s] = proj_mm_last
            return
        yield
        for p in range(NPAIR):
            t_sm = S.op("tensor",
                        (lambda p: (lambda e: e.matmul(ps[7][:, 0:SC], c_sel[0:16, 128 * p:128 * (p + 1)],
                                                       co_s[0:16, :], start=True, stop=True)))(p),
                        waits=[t_co, last], inc=("sT", 1))
            t_cx = S.op("vector", lambda e: e.tensor_copy(cox_s[:, :], ps[7][:, 0:SC]),
                        waits=[t_sm], inc=("sV", 1))
            t_xb = S.op("vector",
                        (lambda p, par: (lambda e: e.tensor_tensor(
                            xbco_s[par][:, SC * p:SC * (p + 1)], xbt_s[par][:, SC * p:SC * (p + 1)],
                            cox_s[:, :], op=ALU.mult)))(p, par),
                        waits=[t_cx, ev_all[p]], inc=("sV", 1))
            t_xc = S.op("vector",
                        (lambda p, par: (lambda e: e.tensor_tensor(
                            xcco_s[par][:, SC * p:SC * (p + 1)], xct_s[par][:, SC * p:SC * (p + 1)],
                            cox_s[:, :], op=ALU.mult)))(p, par),
                        waits=[t_xb, ev_all[NPAIR + p]], inc=("sV", 1))
            last = t_xc
            yield
        x_ready[s] = last
        proj_done[s] = t_ilr

    def gen_scan(s):
        nonlocal w_ready, last_z_evac
        par = s % 2
        xb, xc = xbt_s[par], xct_s[par]
        xa, xbcov, xccov = xat_s[par], xbco_s[par], xcco_s[par]
        s3_last = None
        s1_last = None
        for gi in range(NGRP):
            if DBG and gi > 0:
                break
            gI = s * NGRP + gi
            tc = GT * gi

            def xsl(t, h, tc=tc):
                sl, p = h % 2, h // 2
                return t[64 * sl:64 * sl + 64, SC * p + tc:SC * p + tc + GT]

            mm1 = []
            for h in range(NHL := 16):
                sl, p = h % 2, h // 2
                pb, pf = 64 * sl, 64 * p

                def f(e, h=h, pb=pb, pf=pf, xsl=xsl):
                    st, sk, sp = h < 2, h % 2 == 1, h >= 14
                    e.matmul(ps[2][pb:pb + 64, pf:pf + 64], xsl(xb, h),
                             g_w[pb:pb + 64, pf:pf + 64], start=st, stop=False,
                             skip_group_check=sk)
                    e.matmul(ps[2][pb:pb + 64, pf:pf + 64], xsl(xa, h),
                             c_negi[pb:pb + 64, :], start=False, stop=sp,
                             skip_group_check=sk)
                    e.matmul(ps[3][pb:pb + 64, pf:pf + 64], xsl(xbcov, h),
                             xsl(xb, h), start=st, stop=sp, skip_group_check=sk)
                    e.matmul(ps[4][pb:pb + 64, pf:pf + 64], xsl(xb, h),
                             xsl(xbcov, h), start=st, stop=sp, skip_group_check=sk)
                    return e.matmul(ps[5][pb:pb + 64, pf:pf + 64], xsl(xb, h),
                                    xsl(xccov, h), start=st, stop=sp, skip_group_check=sk)

                w = [x_ready[s], w_ready, last_z_evac] if h == 0 else []
                mm1.append(S.op("tensor", f, waits=w, inc=("sT", 1) if h == 15 else None))
            t_mm1 = mm1[-1]
            t_ltm = S.op("vector", lambda e: e.tensor_tensor(g_ltm[:, :], ps[3][:, :], c_mlt[:, :], op=ALU.mult),
                         waits=[t_mm1], inc=("sV", 1))
            t_lm = S.op("vector", lambda e: e.tensor_tensor(g_lm[:, :], ps[4][:, :], c_mlm[:, :], op=ALU.mult),
                        waits=[t_mm1], inc=("sV", 1))
            t_att = S.op("vector", lambda e: e.tensor_tensor(g_att[:, :], ps[5][:, :], c_mat[:, :], op=ALU.mult),
                         waits=[t_mm1], inc=("sV", 1))
            t_u = S.op("scalar", lambda e: e.activation(g_u[:, :], ps[2][:, :], AF.Copy),
                       waits=[t_mm1], inc=("sS", 1))
            yield

            mm2 = []
            for h in range(16):
                sl, p = h % 2, h // 2
                pb, pf = 64 * sl, 64 * p

                def f(e, h=h, pb=pb, pf=pf, xsl=xsl):
                    st, sk, sp = h < 2, h % 2 == 1, h >= 14
                    e.matmul(ps[4][pb:pb + 64, pf:pf + 64], xsl(xbcov, h),
                             xsl(xc, h), start=st, stop=sp, skip_group_check=sk)
                    e.matmul(ps[3][pb:pb + 64, pf:pf + 64], xsl(xbcov, h),
                             c_negi[pb:pb + 64, :], start=st, stop=sp, skip_group_check=sk)
                    return e.matmul(ps[5][pb:pb + 64, pf:pf + 64], g_lm[pb:pb + 64, pf:pf + 64],
                                    g_ltm[pb:pb + 64, pf:pf + 64], start=st, stop=sp,
                                    skip_group_check=sk)

                w = [t_ltm, t_lm, t_att, t_u] if h == 0 else []
                mm2.append(S.op("tensor", f, waits=w, inc=("sT", 1) if h == 15 else None))
            t_mm2 = mm2[-1]
            s1_last = t_mm2
            t_r1 = S.op("vector", lambda e: e.tensor_tensor(g_r1[:, :], ps[4][:, :], c_mr1[:, :], op=ALU.mult),
                        waits=[t_mm2], inc=("sV", 1))
            t_xbn = S.op("scalar", lambda e: e.activation(g_xbn[:, :], ps[3][:, :], AF.Copy),
                         waits=[t_mm2], inc=("sS", 1))
            t_lt2 = S.op("scalar", lambda e: e.activation(g_lt2[:, :], ps[5][:, :], AF.Copy),
                         waits=[t_mm2], inc=("sS", 1))
            yield

            mm3 = []
            for h in range(16):
                sl, p = h % 2, h // 2
                pb, pf = 64 * sl, 64 * p

                def f(e, h=h, pb=pb, pf=pf):
                    return e.matmul(ps[2][pb:pb + 64, pf:pf + 64], g_lt2[pb:pb + 64, pf:pf + 64],
                                    g_u[pb:pb + 64, pf:pf + 64], start=False, stop=False,
                                    skip_group_check=True)

                mm3.append(S.op("tensor", f, waits=[t_lt2, t_u] if h == 0 else [],
                                inc=("sT", 1) if h == 15 else None))
            t_t1n = S.op("scalar", lambda e: e.activation(g_t1n[:, :], ps[2][:, :], AF.Copy, scale=-1.0),
                         waits=[mm3[-1]], inc=("sS", 1))
            yield
            mm4 = []
            for h in range(16):
                sl, p = h % 2, h // 2
                pb, pf = 64 * sl, 64 * p

                def f(e, h=h, pb=pb, pf=pf):
                    return e.matmul(ps[2][pb:pb + 64, pf:pf + 64], g_ltm[pb:pb + 64, pf:pf + 64],
                                    g_t1n[pb:pb + 64, pf:pf + 64], start=False, stop=(h >= 14),
                                    skip_group_check=True)

                mm4.append(S.op("tensor", f, waits=[t_t1n] if h == 0 else [],
                                inc=("sT", 1) if h == 15 else None))
            t_z1 = S.op("scalar", lambda e: e.activation(g_z1[:, :], ps[2][:, :], AF.Copy),
                        waits=[mm4[-1]], inc=("sS", 1))
            yield

            mm5 = []
            for h in range(16):
                sl, p = h % 2, h // 2
                pb, pf = 64 * sl, 64 * p

                def f(e, h=h, pb=pb, pf=pf, xsl=xsl, gI=gI):
                    st, sk, sp = h < 2, h % 2 == 1, h >= 14
                    e.matmul(ps[2][pb:pb + 64, pf:pf + 64], g_w[pb:pb + 64, pf:pf + 64],
                             xsl(xc, h), start=st, stop=False, skip_group_check=sk)
                    e.matmul(ps[2][pb:pb + 64, pf:pf + 64], g_z1[pb:pb + 64, pf:pf + 64],
                             g_att[pb:pb + 64, pf:pf + 64], start=False, stop=False,
                             skip_group_check=sk)
                    e.matmul(ps[2][pb:pb + 64, pf:pf + 64], g_z1[pb:pb + 64, pf:pf + 64],
                             g_r1[pb:pb + 64, pf:pf + 64], start=False, stop=sp,
                             skip_group_check=sk)
                    return e.matmul(ps[6][pb:pb + 64, pf:pf + 64], g_xbn[pb:pb + 64, pf:pf + 64],
                                    g_z1[pb:pb + 64, pf:pf + 64],
                                    start=(gI == 0 and h < 2), stop=(h >= 14),
                                    skip_group_check=(sk or gI > 0))

                w = [t_z1, t_xbn, t_r1] if h == 0 else []
                mm5.append(S.op("tensor", f, waits=w, inc=("sT", 1) if h == 15 else None))
            t_mm5 = mm5[-1]
            s3_last = t_mm5
            zpar = gI % 2
            t_zt = S.op("vector",
                        (lambda zpar: (lambda e: e.tensor_copy(zst[zpar][:, :], ps[2][:, :])))(zpar),
                        waits=[t_mm5, zdma[-2] if len(zdma) >= 2 else None], inc=("sV", 1))
            last_z_evac = t_zt
            zdst = zallq[s // 4].ap().rearrange("r (sc p tau) -> r sc p tau", sc=4, p=NPAIR)[
                :, s % 4, :, GT * gi:GT * (gi + 1)]
            zsrc = zst[zpar][:, :].rearrange("r (p tau) -> r p tau", p=NPAIR)
            zdma.append(S.op("gpsimd", (lambda zd, zs: (lambda e: e.dma_start(out=zd, in_=zs)))(zdst, zsrc),
                             waits=[t_zt], inc=("dZ%d" % zpar, 16)))
            w_ready = S.op("vector", lambda e: e.tensor_tensor(g_w[:, :], c_w1[:, :], ps[6][:, :], op=ALU.add),
                           waits=[t_mm5], inc=("sV", 1))
            yield
        scan_s3_done[s] = s3_last
        scan_s1_done[s] = s1_last
        if s % 4 == 3:
            q = s // 4
            coll_ticks.append(S.op("gpsimd",
                (lambda q: (lambda e: e.collective_compute(
                    "AllGather", ALU.bypass,
                    replica_groups=[[0, 1], [2, 3], [4, 5], [6, 7]],
                    ins=[zallq[q].ap().opt()], outs=[zagoq[q].ap().opt()])))(q),
                waits=[zdma[-1]], inc=("sC", 1)))

    if DBG:
        if DBG_LEVEL >= 2:
            emit_ht_load(0)
        if DBG_LEVEL >= 3:
            emit_proj(0)
        if DBG_LEVEL >= 4:
            emit_scan(0)
        dumps = [wk_s[:, 15 * 1024:16 * 1024]]
        if DBG_LEVEL >= 2:
            dumps += [ht_s[0]]
        if DBG_LEVEL >= 3:
            dumps += [xbt_s[0], xct_s[0], xat_s, xbco_s, xcco_s]
        if DBG_LEVEL >= 4:
            dumps += [g_u, g_lm, g_ltm, g_att, g_r1, g_xbn, g_lt2, g_t1n, g_z1,
                      zst[0], g_w]
        off = 0
        t_dmp = None
        for dsrc in dumps:
            ncol = dsrc.shape[-1]
            nrow = dsrc.shape[0]
            dwaits = [tw_q, tw_k, tw_v, tw_i, t_wsb0, t_dmp]
            if DBG_LEVEL >= 2:
                dwaits.append(ht_ready[0])
            if DBG_LEVEL >= 3:
                dwaits.append(x_ready[0])
            if DBG_LEVEL >= 4:
                dwaits += [w_ready, last_z_evac, zdma[-1]]
            t_dmp = S.op("gpsimd",
                         (lambda dsrc, off, ncol, nrow: (lambda e: e.dma_start(
                             out=dbg_d.ap()[0:nrow, off:off + ncol], in_=dsrc)))(dsrc, off, ncol, nrow),
                         waits=dwaits, inc=("dZ", 16))
            off += ncol
        with nc.Block() as blk:
            @blk.tensor
            def _(eng):
                S.emit("tensor", eng, sems)

            @blk.vector
            def _(eng):
                S.emit("vector", eng, sems)

            @blk.scalar
            def _(eng):
                S.emit("scalar", eng, sems)

            @blk.sync
            def _(eng):
                S.emit("sync", eng, sems)

            @blk.gpsimd
            def _(eng):
                S.emit("gpsimd", eng, sems)
                eng.wait_ge(sems["dZ"], t_dmp[1])
        return nc

    emit_ht_load(0)
    for _ in gen_proj(0):
        pass
    wot_ticks = []
    for s in range(NSC):
        if s == NSC - 1:
            wsrc = wot_d.ap().rearrange("(k r) j -> r k j", r=128)
            wdst = wot_s.rearrange("r (k j) -> r k j", k=16)
            for i in range(4):
                wot_ticks.append(S.op("gpsimd", dma(wdst[:, 4 * i:4 * (i + 1), :], wsrc[:, 4 * i:4 * (i + 1), :]),
                                     waits=[proj_done[NSC - 1]], inc=("dB", 16)))
        if s + 1 < NSC:
            emit_ht_load(s + 1)
        gs = gen_scan(s)
        gp = gen_proj(s + 1) if s + 1 < NSC else None
        k = 0
        while True:
            try:
                next(gs)
            except StopIteration:
                break
            if gp is not None:
                for _ in range(2 if k % 3 != 2 else 1):
                    try:
                        next(gp)
                    except StopIteration:
                        gp = None
                        break
            k += 1
        if gp is not None:
            for _ in gp:
                pass

    import os
    if os.environ.get("KSERIAL", "0") == "1":
        S.barrier_mode = True

    t_wot = wot_ticks[-1]
    p1_free2 = [proj_done[NSC - 1], scan_s3_done[NSC - 1], scan_s1_done[NSC - 1],
                w_ready, last_z_evac, x_ready[NSC - 1]]

    t_wmax = S.op("vector", lambda e: e.memset(wmx_s[:, 0:1], 0.0), inc=("sV", 1))
    fw_prev = [None, None]
    for k in range(16):
        sp = k % 2
        t_l = S.op("sync", dma(fw_stg[sp][:, :], fwt_d.ap()[128 * k:128 * (k + 1), :]),
                   waits=[fw_prev[sp]], inc=("dF%d" % sp, 16))
        t_m = S.op("vector",
                   (lambda sp: (lambda e: e.reduce_max(wmx_s[:, 1:2], fw_stg[sp][:, :], axis=AXX,
                                                       apply_absolute_value=True)))(sp),
                   waits=[t_l, t_wmax], inc=("sV", 1))
        t_wmax = S.op("vector", lambda e: e.tensor_tensor(wmx_s[:, 0:1], wmx_s[:, 0:1], wmx_s[:, 1:2], op=ALU.max),
                      waits=[t_m], inc=("sV", 1))
        fw_prev[sp] = t_wmax

    t_mxacc = S.op("vector", lambda e: e.memset(mx_s[:, 0:1], 0.0), inc=("sV", 1))
    x1_ticks = []
    prev_sq = None
    hist = []

    def zview(bufs, row, X, tt):
        sc = X * (NSC // 2) + tt // 2
        v = bufs[sc // 4].ap().rearrange("r (c p tau) -> r c p tau", p=NPAIR, tau=SC)
        return v[row:row + 128, sc % 4, :, (tt % 2) * 128:(tt % 2) * 128 + 128]

    def zcoll(X, tt):
        return coll_ticks[(X * (NSC // 2) + tt // 2) // 4]

    import os as _os
    _G1 = _os.environ.get("KG", "")
    pend = {}

    def emit_part1(tt):
        par = tt % 2
        zpar4 = tt % 4
        h4 = hist[tt - 4] if tt >= 4 else None
        guard = h4["mm"] if h4 else None
        gw = [guard] if h4 else list(p1_free2)
        sem = "dC%d" % par
        t_l1 = S.op("sync", dma(ztl_s[zpar4][:, :].rearrange("r (p c) -> r p c", p=NPAIR), zview(zallq, 0, 0, tt)),
                    waits=gw, inc=(sem, 16))
        t_l2 = S.op("sync", dma(ztt_s[zpar4][:, :].rearrange("r (p c) -> r p c", p=NPAIR), zview(zallq, 0, 1, tt)),
                    waits=gw, inc=(sem, 16))
        t_p1 = S.op("sync", dma(ztp_s[zpar4][:, :].rearrange("r (p c) -> r p c", p=NPAIR), zview(zagoq, 128, 0, tt)),
                    waits=[zcoll(0, tt)] + gw, inc=(sem, 16))
        t_hr = S.op("sync", dma(hr_s[zpar4][:, :], hres_d.ap()[128 * tt:128 * (tt + 1), :]),
                    waits=gw, inc=(sem, 16))
        t_o1 = S.op("vector",
                    (lambda q4: (lambda e: e.tensor_scalar(ztl_s[q4][:, :], ztl_s[q4][:, :],
                                                           msel_s[:, 0:1], None, op0=ALU.mult)))(zpar4),
                    waits=[t_hr], inc=("sV", 1))
        t_own = S.op("vector",
                     (lambda q4: (lambda e: e.scalar_tensor_tensor(ztl_s[q4][:, :], ztt_s[q4][:, :],
                                                                   msel_s[:, 1:2], ztl_s[q4][:, :],
                                                                   op0=ALU.mult, op1=ALU.add)))(zpar4),
                     waits=[t_o1], inc=("sV", 1))
        t_q1 = S.op("vector",
                    (lambda q4: (lambda e: e.tensor_scalar(ztp_s[q4][:, :], ztp_s[q4][:, :],
                                                           msel_s[:, 2:3], None, op0=ALU.mult)))(zpar4),
                    waits=[t_hr], inc=("sV", 1))
        t_p2 = S.op("sync", dma(ztt_s[zpar4][:, :].rearrange("r (p c) -> r p c", p=NPAIR), zview(zagoq, 0, 1, tt)),
                    waits=[t_own, zcoll(1, tt)], inc=(sem, 16))
        t_par = S.op("vector",
                     (lambda q4: (lambda e: e.scalar_tensor_tensor(ztp_s[q4][:, :], ztt_s[q4][:, :],
                                                                   msel_s[:, 3:4], ztp_s[q4][:, :],
                                                                   op0=ALU.mult, op1=ALU.add)))(zpar4),
                     waits=[t_q1, t_p2], inc=("sV", 1))
        pend[tt] = (t_hr, t_own, t_par)

    def emit_part2(tt):
        nonlocal prev_sq, t_mxacc
        par = tt % 2
        zpar4 = tt % 4
        h2 = hist[tt - 2] if tt >= 2 else None
        t_hr, t_own, t_par = pend.pop(tt)
        mmt = []
        for ob in range(4):
            def f(e, ob=ob, par=par, tt=tt):
                bank = ps[par * 4 + ob]
                mm = None
                for jj in range(16):
                    lhs = (ztl_s[tt % 4] if jj < 8 else ztp_s[tt % 4])[:, 128 * (jj % 8):128 * (jj % 8) + 128]
                    mm = e.matmul(bank[:, :], lhs,
                                  wot_s[:, 2048 * jj + 512 * ob:2048 * jj + 512 * ob + 512],
                                  start=(jj == 0), stop=(jj == 15))
                return mm

            w = [t_own, t_par, t_wot, h2["ev"] if h2 else None]
            if tt == 0:
                w += [w_ready, last_z_evac]
            mmt.append(S.op("tensor", f, waits=w, inc=("sT", 1)))
        ev = []
        for ob in range(4):
            def f(e, ob=ob, par=par, q4=zpar4):
                return e.tensor_tensor(x1p_s[par][:, 512 * ob:512 * (ob + 1)],
                                       ps[par * 4 + ob][:, :],
                                       hr_s[q4][:, 512 * ob:512 * (ob + 1)], op=ALU.add)

            w = [mmt[ob], t_hr]
            if h2:
                w += [h2["x1"], h2["sqr"]]
            ev.append(S.op("vector", f, waits=w, inc=("sV", 1)))

        t_sqr = S.op("scalar",
                     (lambda par: (lambda e: e.activation(sq2_s[par][:, :], x1p_s[par][:, :], AF.Square,
                                                          accum_out=st_s[:, 1:2])))(par),
                     waits=ev + [prev_sq], inc=("sS", 1))
        t_st = S.op("vector",
                    (lambda par: (lambda e: e.reduce_sum(st_s[:, 0:1], x1p_s[par][:, :], axis=AXX)))(par),
                    waits=ev + [prev_sq], inc=("sV", 1))
        t_mn = S.op("vector", lambda e: e.tensor_scalar(st_s[:, 24:25], st_s[:, 0:1], 1.0 / D, None, op0=ALU.mult),
                    waits=[t_st], inc=("sV", 1))
        t_m2 = S.op("vector", lambda e: e.tensor_tensor(st_s[:, 29:30], st_s[:, 24:25], st_s[:, 24:25], op=ALU.mult),
                    waits=[t_mn], inc=("sV", 1))
        t_vr = S.op("vector", lambda e: e.scalar_tensor_tensor(st_s[:, 25:26], st_s[:, 1:2], 1.0 / D,
                                                               st_s[:, 29:30], op0=ALU.mult, op1=ALU.subtract),
                    waits=[t_m2, t_sqr], inc=("sV", 1))
        t_sq = S.op("scalar", lambda e: e.activation(st_s[:, 26:27], st_s[:, 25:26], AF.Sqrt, bias=eps_s[:, 0:1]),
                    waits=[t_vr, t_eps], inc=("sS", 1))
        prev_sq = t_sq
        t_rc = S.op("vector", lambda e: e.reciprocal(st_s[:, 27:28], st_s[:, 26:27]),
                    waits=[t_sq], inc=("sV", 1))
        t_nm = S.op("vector", lambda e: e.scalar_tensor_tensor(st_s[:, 28:29], st_s[:, 24:25], -1.0,
                                                               st_s[:, 27:28], op0=ALU.mult, op1=ALU.mult),
                    waits=[t_rc], inc=("sV", 1))
        t_x1 = S.op("vector",
                    (lambda par: (lambda e: e.tensor_scalar(x1b_s[par][:, :], x1p_s[par][:, :],
                                                            st_s[:, 27:28], st_s[:, 28:29],
                                                            op0=ALU.mult, op1=ALU.add)))(par),
                    waits=[t_nm, h2["xd"] if h2 else None], inc=("sV", 1))
        t_m = S.op("vector",
                   (lambda par: (lambda e: e.reduce_max(mx_s[:, 1:2], x1b_s[par][:, :], axis=AXX,
                                                        apply_absolute_value=True)))(par),
                   waits=[t_x1, t_mxacc], inc=("sV", 1))
        t_mxacc = S.op("vector", lambda e: e.tensor_tensor(mx_s[:, 0:1], mx_s[:, 0:1], mx_s[:, 1:2], op=ALU.max),
                       waits=[t_m], inc=("sV", 1))
        t_xd = S.op("gpsimd",
                    (lambda par, tt: (lambda e: e.dma_start(out=x1scr.ap()[128 * tt:128 * (tt + 1), :],
                                                            in_=x1b_s[par][:, :])))(par, tt),
                    waits=[t_x1], inc=("dX", 16))
        x1_ticks.append(t_xd)
        hist.append({"mm": mmt[-1], "ev": ev[-1], "x1": t_x1, "sqr": t_sqr, "xd": t_xd})

    LEAD = 2
    for tt in range(NTT):
        emit_part1(tt)
        if tt >= LEAD:
            emit_part2(tt - LEAD)
    for tt in range(NTT - LEAD, NTT):
        emit_part2(tt)

    tr_ticks = []
    for k in range(4):
        tr_ticks.append(S.op("sync",
                             (lambda k: (lambda e: e.dma_start_transpose(
                                 x1t[:, 2048 * k:2048 * (k + 1)], x1scr.ap()[:, 128 * k:128 * (k + 1)])))(k),
                             waits=[x1_ticks[-1]], inc=("dT%d" % (k % 2), 16)))

    if os.environ.get("KSERIAL", "0") == "3":
        S.barrier_mode = True
    t_mxd = S.op("gpsimd", lambda e: e.dma_start(out=mx_d.ap().rearrange("o c -> c o"), in_=mx_s[:, 0:1]),
                 waits=[t_mxacc], inc=("dM", 16))
    t_ar = S.op("gpsimd",
                lambda e: e.collective_compute("AllReduce", ALU.max, replica_groups=[list(range(8))],
                                               ins=[mx_d.ap().opt()], outs=[mxo_d.ap().opt()]),
                waits=[t_mxd], inc=("sC", 1))
    t_l = S.op("sync", dma(sc_s[0:8, 16:32], mxo_d.ap().rearrange("o (a b) -> (o a) b", b=16)),
               waits=[t_ar], inc=("dA", 16))
    t_r = S.op("vector", lambda e: e.reduce_max(sc_s[0:8, 32:33], sc_s[0:8, 16:32], axis=AXX),
               waits=[t_l], inc=("sV", 1))
    t_d = S.op("gpsimd", lambda e: e.dma_start(out=mx_d.ap()[0:1, 0:8].rearrange("o c -> c o"),
                                               in_=sc_s[0:8, 32:33]),
               waits=[t_r], inc=("dM", 16))
    t_l2 = S.op("sync", dma(sc_s[0:1, 33:41], mx_d.ap()[0:1, 0:8]), waits=[t_d], inc=("dA", 16))
    t_xs = S.op("vector", lambda e: e.reduce_max(sc_s[0:1, 8:9], sc_s[0:1, 33:41], axis=AXX),
                waits=[t_l2], inc=("sV", 1))
    t_xs = S.op("vector", lambda e: e.tensor_scalar(sc_s[0:1, 8:9], sc_s[0:1, 8:9], 1e-6, None, op0=ALU.max),
                waits=[t_xs], inc=("sV", 1))
    t_wd = S.op("gpsimd", lambda e: e.dma_start(out=mx_d.ap().rearrange("o c -> c o"),
                                                in_=wmx_s[:, 0:1]),
                waits=[t_wmax, t_l, t_l2], inc=("dM", 16))
    t_wl = S.op("sync", dma(sc_s[0:8, 16:32], mx_d.ap().rearrange("o (a b) -> (o a) b", b=16)),
                waits=[t_wd, t_r], inc=("dA", 16))
    t_wr = S.op("vector", lambda e: e.reduce_max(sc_s[0:8, 32:33], sc_s[0:8, 16:32], axis=AXX),
                waits=[t_wl, t_xs], inc=("sV", 1))
    t_wd2 = S.op("gpsimd", lambda e: e.dma_start(out=mx_d.ap()[0:1, 0:8].rearrange("o c -> c o"),
                                                 in_=sc_s[0:8, 32:33]),
                 waits=[t_wr, t_l2], inc=("dM", 16))
    t_wl2 = S.op("sync", dma(sc_s[0:1, 33:41], mx_d.ap()[0:1, 0:8]), waits=[t_wd2], inc=("dA", 16))
    t_ws = S.op("vector", lambda e: e.reduce_max(sc_s[0:1, 9:10], sc_s[0:1, 33:41], axis=AXX),
                waits=[t_wl2], inc=("sV", 1))
    t_ws = S.op("vector", lambda e: e.tensor_scalar(sc_s[0:1, 9:10], sc_s[0:1, 9:10], 1e-6, None, op0=ALU.max),
                waits=[t_ws], inc=("sV", 1))
    t_rws = S.op("vector", lambda e: e.reciprocal(sc_s[0:1, 11:12], sc_s[0:1, 9:10]),
                 waits=[t_ws], inc=("sV", 1))
    t_bcwm = S.op("tensor", lambda e: e.matmul(ps[7][:, 4:5], onesv_s[0:1, 0:128],
                                               sc_s[0:1, 11:12], start=True, stop=True),
                  waits=[t_rws, t_const], inc=("sT", 1))
    t_bcw = S.op("vector", lambda e: e.tensor_copy(sc_s[:, 21:22], ps[7][:, 4:5]),
                 waits=[t_bcwm], inc=("sV", 1))
    t_rxs = S.op("vector", lambda e: e.reciprocal(sc_s[0:1, 10:11], sc_s[0:1, 8:9]),
                 waits=[t_xs], inc=("sV", 1))
    t_xw = S.op("vector", lambda e: e.tensor_tensor(sc_s[0:1, 12:13], sc_s[0:1, 8:9],
                                                    sc_s[0:1, 9:10], op=ALU.mult),
                waits=[t_xs, t_ws], inc=("sV", 1))
    t_bcm = S.op("tensor", lambda e: e.matmul(ps[7][:, 0:3], onesv_s[0:1, 0:128],
                                              sc_s[0:1, 10:13], start=True, stop=True),
                 waits=[t_rxs, t_xw, t_const], inc=("sT", 1))
    t_bc = S.op("vector", lambda e: e.tensor_copy(sc_s[:, 24:27], ps[7][:, 0:3]),
                waits=[t_bcm], inc=("sV", 1))

    if os.environ.get("KSERIAL", "0") == "4":
        S.barrier_mode = True
    wq_ticks = []
    for k in range(16):
        sp = k % 2
        t_lw = S.op("sync", dma(fw_stg[sp][:, :], fwt_d.ap()[128 * k:128 * (k + 1), :]),
                    waits=[wq_ticks[-2] if len(wq_ticks) >= 2 else t_bcw], inc=("dF%d" % sp, 16))
        t_q = S.op("scalar",
                   (lambda sp, k: (lambda e: e.activation(wq8_s[:, 2048 * k:2048 * (k + 1)],
                                                          fw_stg[sp][:, :], AF.Copy,
                                                          scale=sc_s[:, 21:22])))(sp, k),
                   waits=[t_lw, t_bcw], inc=("sS", 1))
        wq_ticks.append(t_q)
    t_wq = wq_ticks[-1]

    xq_ticks = []
    for k in range(16):
        xpar = k % 4
        if k < 4:
            t_t = tr_ticks[k]
        else:
            t_t = S.op("sync",
                       (lambda k, xpar: (lambda e: e.dma_start_transpose(
                           x1t[:, 2048 * xpar:2048 * (xpar + 1)], x1scr.ap()[:, 128 * k:128 * (k + 1)])))(k, xpar),
                       waits=[xq_ticks[-4]],
                       inc=("dT%d" % (xpar % 2), 16))
        eng = "scalar" if k % 2 == 0 else "vector"
        if eng == "scalar":
            t_q = S.op(eng,
                       (lambda k, xpar: (lambda e: e.activation(xq8_s[:, 2048 * k:2048 * (k + 1)],
                                                                x1t[:, 2048 * xpar:2048 * (xpar + 1)], AF.Copy,
                                                                scale=sc_s[:, 24:25])))(k, xpar),
                       waits=[t_t, t_bc], inc=("sS", 1))
        else:
            t_q = S.op(eng,
                       (lambda k, xpar: (lambda e: e.tensor_scalar(xq8_s[:, 2048 * k:2048 * (k + 1)],
                                                                   x1t[:, 2048 * xpar:2048 * (xpar + 1)],
                                                                   sc_s[:, 24:25], None, op0=ALU.mult)))(k, xpar),
                       waits=[t_t, t_bc], inc=("sV", 1))
        xq_ticks.append(t_q)
    t_xq = xq_ticks[-1]
    t_xq2 = xq_ticks[-2]

    out_ticks = []
    prev_sq4 = None
    for tt in range(NTT):
        par = tt % 2
        guard = out_ticks[-2] if len(out_ticks) >= 2 else None
        t_x1r = S.op("sync", dma(x1r_s[par][:, :], x1scr.ap()[128 * tt:128 * (tt + 1), :]),
                     waits=[x1_ticks[-1], guard], inc=("dR%d" % par, 16))
        mmt = []
        for ob in range(4):
            def f(e, ob=ob, par=par, tt=tt):
                bank = ps[par * 4 + ob]
                mm = None
                for k in range(16):
                    mm = e.matmul(bank[:, :],
                                  xq8_s[:, 2048 * k + 128 * tt:2048 * k + 128 * tt + 128],
                                  wq8_s[:, 2048 * k + 512 * ob:2048 * k + 512 * ob + 512],
                                  start=(k == 0), stop=(k == 15))
                return mm

            w = [t_wq, t_xq, t_xq2, guard]
            if tt == 0:
                w.append(t_bcm)
            mmt.append(S.op("tensor", f, waits=w, inc=("sT", 1)))
        ev = []
        for ob in range(4):
            def f1(e, ob=ob, par=par):
                return e.tensor_scalar(y_s[par][:, 512 * ob:512 * (ob + 1)], ps[par * 4 + ob][:, :],
                                       sc_s[:, 26:27], None, op0=ALU.mult)

            t_y = S.op("vector", f1, waits=[mmt[ob], t_bc], inc=("sV", 1))

            def f2(e, ob=ob, par=par):
                return e.tensor_tensor(y_s[par][:, 512 * ob:512 * (ob + 1)],
                                       y_s[par][:, 512 * ob:512 * (ob + 1)],
                                       fb_s[:, 512 * ob:512 * (ob + 1)], op=ALU.add)

            t_y2 = S.op("vector", f2, waits=[t_y, t_const], inc=("sV", 1))

            def f3(e, ob=ob, par=par):
                return e.tensor_tensor(x2p_s[par][:, 512 * ob:512 * (ob + 1)],
                                       y_s[par][:, 512 * ob:512 * (ob + 1)],
                                       x1r_s[par][:, 512 * ob:512 * (ob + 1)], op=ALU.add)

            ev.append(S.op("vector", f3, waits=[t_y2, t_x1r], inc=("sV", 1)))

        t_sqr = S.op("scalar",
                     (lambda par: (lambda e: e.activation(sq4_s[par][:, :], x2p_s[par][:, :], AF.Square,
                                                          accum_out=st4_s[:, 1:2])))(par),
                     waits=ev + [prev_sq4], inc=("sS", 1))
        t_st = S.op("vector",
                    (lambda par: (lambda e: e.reduce_sum(st4_s[:, 0:1], x2p_s[par][:, :], axis=AXX)))(par),
                    waits=ev + [prev_sq4], inc=("sV", 1))
        t_mn = S.op("vector", lambda e: e.tensor_scalar(st4_s[:, 24:25], st4_s[:, 0:1], 1.0 / D, None, op0=ALU.mult),
                    waits=[t_st], inc=("sV", 1))
        t_m2 = S.op("vector", lambda e: e.tensor_tensor(st4_s[:, 29:30], st4_s[:, 24:25], st4_s[:, 24:25], op=ALU.mult),
                    waits=[t_mn], inc=("sV", 1))
        t_vr = S.op("vector", lambda e: e.scalar_tensor_tensor(st4_s[:, 25:26], st4_s[:, 1:2], 1.0 / D,
                                                               st4_s[:, 29:30], op0=ALU.mult, op1=ALU.subtract),
                    waits=[t_m2, t_sqr], inc=("sV", 1))
        t_sq = S.op("scalar", lambda e: e.activation(st4_s[:, 26:27], st4_s[:, 25:26], AF.Sqrt, bias=eps_s[:, 0:1]),
                    waits=[t_vr, t_eps], inc=("sS", 1))
        prev_sq4 = t_sq
        t_rc = S.op("vector", lambda e: e.reciprocal(st4_s[:, 27:28], st4_s[:, 26:27]),
                    waits=[t_sq], inc=("sV", 1))
        t_nm = S.op("vector", lambda e: e.scalar_tensor_tensor(st4_s[:, 28:29], st4_s[:, 24:25], -1.0,
                                                               st4_s[:, 27:28], op0=ALU.mult, op1=ALU.mult),
                    waits=[t_rc], inc=("sV", 1))
        t_o = S.op("vector",
                   (lambda par: (lambda e: e.tensor_scalar(o_s[par][:, :], x2p_s[par][:, :],
                                                           st4_s[:, 27:28], st4_s[:, 28:29],
                                                           op0=ALU.mult, op1=ALU.add)))(par),
                   waits=[t_nm], inc=("sV", 1))
        t_od = S.op("gpsimd",
                    (lambda par, tt: (lambda e: e.dma_start(out=out_d.ap()[128 * tt:128 * (tt + 1), :],
                                                            in_=o_s[par][:, :])))(par, tt),
                    waits=[t_o], inc=("dO", 16))
        out_ticks.append(t_od)

    final = out_ticks[-1]

    if DBG2:
        zall_o = nc.dram_tensor("zall_o", [128, NSC * 2048], BF16, kind="ExternalOutput")
        x1_o = nc.dram_tensor("x1_o", [TOK, D], BF16, kind="ExternalOutput")
        scl_o = nc.dram_tensor("scl_o", [128, 64], F32, kind="ExternalOutput")
        zago_o = nc.dram_tensor("zago_o", [256, NSC * 2048], BF16, kind="ExternalOutput")
        t_d1 = S.op("gpsimd", lambda e: e.dma_start(out=zall_o.ap(), in_=zall.ap()),
                    waits=[zdma[-1], out_ticks[-1]], inc=("dZ", 16))
        t_d2 = S.op("gpsimd", lambda e: e.dma_start(out=x1_o.ap(), in_=x1scr.ap()),
                    waits=[t_d1], inc=("dZ", 16))
        t_d3 = S.op("gpsimd", lambda e: e.dma_start(out=zago_o.ap(), in_=zago.ap()),
                    waits=[t_d2], inc=("dZ", 16))
        t_d4 = S.op("gpsimd", lambda e: e.dma_start(out=scl_o.ap(), in_=sc_s[:, 0:64].bitcast(BF16).bitcast(F32)),
                    waits=[t_bc, t_d3], inc=("dZ", 16))
        final = t_d4
        fsem = "dZ"
    else:
        fsem = "dO"

    rec = {} if os.environ.get("KREC", "0") == "1" else None
    with nc.Block() as blk:
        @blk.tensor
        def _(eng):
            S.emit("tensor", eng, sems, nc, rec)

        @blk.vector
        def _(eng):
            S.emit("vector", eng, sems, nc, rec)

        @blk.scalar
        def _(eng):
            S.emit("scalar", eng, sems, nc, rec)

        @blk.sync
        def _(eng):
            S.emit("sync", eng, sems, nc, rec)

        @blk.gpsimd
        def _(eng):
            S.emit("gpsimd", eng, sems, nc, rec)
            eng.wait_ge(sems[fsem], final[1])
    nc._sched = S
    nc._rec = rec

    return nc


def _consts():
    negi = np.tile(-np.eye(64, dtype=np.float32), (2, 1)).astype(BF)
    ci = np.arange(GT) // CS
    up = (ci[None, :] > ci[:, None]).astype(np.float32)
    tri = ((ci[:, None] == ci[None, :]) &
           (np.arange(GT)[:, None] <= np.arange(GT)[None, :])).astype(np.float32)

    def tile(m):
        o = np.zeros((128, 512), np.float32)
        for sl in range(2):
            for p in range(8):
                o[64 * sl:64 * sl + 64, 64 * p:64 * p + 64] = m
        return o.astype(BF)

    mlt, mlm, matt, mr1 = tile(up), tile(up.T), tile(-tri), tile(-up)
    tokf = np.tile((1.0 / ((np.arange(SC) % CS) + 1.0) / HF).astype(np.float32)[None, :], (16, 1))
    sel = np.zeros((16, 1024), np.float32)
    for h in range(16):
        p, sl = h // 2, h % 2
        sel[h, 128 * p + 64 * sl:128 * p + 64 * sl + 64] = 1.0
    return negi, mlt, mlm, matt, mr1, np.ascontiguousarray(tokf), np.ascontiguousarray(sel), np.ones((1, 128), np.float32)


def kernel(L0_attn_hidden_states, L0_attn_Wq, L0_attn_Wk, L0_attn_Wv, L0_attn_Wo,
           L0_attn_W_ilr, L0_attn_b_ilr, L0_attn_W1, L0_ffn_weight, L0_ffn_bias):
    hid = np.ascontiguousarray(np.asarray(L0_attn_hidden_states, np.float32))
    Wq = np.asarray(L0_attn_Wq, np.float32)
    Wk = np.asarray(L0_attn_Wk, np.float32)
    Wv = np.asarray(L0_attn_Wv, np.float32)
    Wo = np.asarray(L0_attn_Wo, np.float32)
    Wilr = np.asarray(L0_attn_W_ilr, np.float32)
    bilr = np.asarray(L0_attn_b_ilr, np.float32)
    W1 = np.asarray(L0_attn_W1, np.float32)
    fw = np.asarray(L0_ffn_weight)
    fb = np.asarray(L0_ffn_bias)

    negi, mlt, mlm, matt, mr1, tokf, sel, ones1 = _consts()
    fwt = np.ascontiguousarray(fw.T)
    fbias = np.ascontiguousarray(np.tile(np.asarray(fb)[None, :], (128, 1)))

    in_maps = []
    for c in range(8):
        b, hh = c // 2, c % 2
        Hs = slice(1024 * hh, 1024 * (hh + 1))
        w1i = np.zeros((128, 512), np.float32)
        for hl in range(16):
            p, sl = hl // 2, hl % 2
            w1i[64 * sl:64 * sl + 64, 64 * p:64 * p + 64] = W1[16 * hh + hl]
        wot = np.ascontiguousarray(np.concatenate(
            [Wo[:, Hs].T, Wo[:, 1024 * (1 - hh):1024 * (2 - hh)].T], axis=0))
        msel = np.zeros((128, 4), np.float32)
        msel[:, 0] = 1.0 if hh == 0 else 0.0
        msel[:, 1] = 1.0 if hh == 1 else 0.0
        msel[:, 2] = 1.0 if hh == 0 else 0.0
        msel[:, 3] = 1.0 if hh == 1 else 0.0
        in_maps.append({
            "hT": np.ascontiguousarray(hid[b].T),
            "hres": np.ascontiguousarray(hid[b][TOK * hh:TOK * (hh + 1), :]),
            "wqt": np.ascontiguousarray(Wq[Hs, :].T),
            "wkt": np.ascontiguousarray(Wk[Hs, :].T),
            "wvt": np.ascontiguousarray(Wv[Hs, :].T),
            "wilrt": np.ascontiguousarray(Wilr[16 * hh:16 * hh + 16, :].T),
            "bilr": np.ascontiguousarray(bilr[16 * hh:16 * hh + 16, None]),
            "w1init": w1i, "wot": wot, "fwt": fwt, "fbias": fbias,
            "negi": negi, "mlt": mlt, "mlm": mlm, "matt": matt, "mr1": mr1,
            "tokf": tokf, "sel": sel, "ones1": ones1, "msel": msel,
        })

    nc = build_nc()
    res = run_bass_kernel_spmd(nc, in_maps, core_ids=list(range(8)))
    out = np.zeros((B, L, D), np.float32)
    for c in range(8):
        b, hh = c // 2, c % 2
        out[b, TOK * hh:TOK * (hh + 1), :] = res.results[c]["out"]
    return out
